# version 19
# baseline (speedup 1.0000x reference)
import numpy as np
import ml_dtypes
from contextlib import ExitStack
import concourse.bass as bass
import concourse.mybir as mybir
from concourse.bass_utils import run_bass_kernel_spmd

F32 = mybir.dt.float32
BF16 = mybir.dt.bfloat16
AF = mybir.ActivationFunctionType
ALU = mybir.AluOpType
AX = mybir.AxisListType

D = 1024
HEADS = 16
TFULL = 8192
TOWN = 2048
CH = 128
NT = TFULL // CH
OWN0 = (TFULL - TOWN) // CH
NOWN = TOWN // CH
DFF = 4096
QEPS = 1e-5
LNEPS = 1e-5
BASE = 16512
SBUF_LIMIT = 229344

DEBUG_TAPS = []
SCAN_LEVEL = 5
PHI_MODE = 2
BUILD_UPTO = None
bfloat16 = ml_dtypes.bfloat16


class Sem:
    def __init__(self, h):
        self.h = h
        self.n = 0

    def tok(self):
        return (self, self.n)


class Builder:
    def __init__(self):
        self.plan = []

    def op(self, eng, fn, waits=(), inc=None, inc_by=1):
        merged = {}
        for t in waits:
            if t is None:
                continue
            s, v = t
            if v > 0 and merged.get(id(s), (None, 0))[1] < v:
                merged[id(s)] = (s, v)
        self.plan.append((eng, fn, list(merged.values()), inc, inc_by))
        if inc is not None:
            inc.n += inc_by
            return inc.tok()
        return None

    def run(self, eng_obj, names):
        if isinstance(names, str):
            names = {names}
        waited = {}
        for (e, fn, waits, inc, by) in self.plan:
            if e not in names:
                continue
            for (s, v) in waits:
                if waited.get(id(s), 0) < v:
                    eng_obj.wait_ge(s.h, v)
                    waited[id(s)] = v
            ins = fn(eng_obj)
            if inc is not None:
                ins.then_inc(inc.h, by)


def build_nc():
    nc = bass.Bass("TRN2", target_bir_lowering=False)

    d_in = {}
    for name, shape, dt in [
        ("x_seq", [TFULL, D], F32), ("x_ownT", [D, TOWN], F32),
        ("wqT", [D, D], F32), ("wkT", [D, D], F32), ("wvT", [D, D], F32), ("woT", [D, D], F32),
        ("w1T", [D, DFF], F32), ("w2T", [DFF, D], F32),
        ("b1m", [128, 32], F32), ("b2m", [128, 8], F32),
        ("maskT", [128, 128], F32), ("cmask", [128, NT], BF16),
        ("onesb", [128, 1], BF16), ("onesf", [128, 1], F32), ("onesfr", [1, 128], F32),
        ("idn", [128, 128], BF16),
    ]:
        d_in[name] = nc.dram_tensor(name, shape, dt, kind="ExternalInput").ap()
    out_f = nc.dram_tensor("out_f", [D, TOWN], F32, kind="ExternalOutput").ap()
    wq1_d = nc.dram_tensor("wq1_d", [D, DFF], BF16, kind="Internal").ap()
    wq2_d = nc.dram_tensor("wq2_d", [DFF, D], BF16, kind="Internal").ap()

    taps = {}
    for name, shape, dt in [
        ("xn", [128, 8 * TOWN], BF16), ("phik", [128, NOWN * 1024], BF16),
        ("v", [128, NOWN * 1024], BF16), ("sloc", [128, NOWN * 520], BF16),
        ("phiq", [128, 8 * TOWN], BF16), ("yt", [128, NOWN * 1024], BF16),
        ("yf", [128, 8 * TOWN], BF16), ("s", [128, 8 * TOWN], BF16),
        ("tvals", [1, 2], F32), ("wq1", [128, 1024], BF16), ("h0", [128, 1024], BF16),
    ]:
        if name in DEBUG_TAPS:
            taps[name] = nc.dram_tensor("tap_" + name, shape, dt, kind="ExternalOutput").ap()

    ctx = ExitStack()
    with ctx:
        def alloc(name, shape, dt, off):
            esz = {F32: 4, BF16: 2}[dt]
            per_part = int(np.prod(shape[1:])) * esz
            assert off >= BASE and off + per_part <= SBUF_LIMIT, f"{name}: {off}+{per_part}"
            return nc.alloc_sbuf_tensor_at(name, list(shape), dt, offset=off)

        A_OFF = BASE
        D_OFF = BASE + 32768
        E_OFF = BASE + 65536
        F_OFF = BASE + 98304
        G_OFF = BASE + 131072
        C_OFF = BASE + 163840
        M_OFF = BASE + 180480

        xn_own = alloc("xn_own", [128, 8 * TOWN], BF16, A_OFF)
        phik_own = alloc("phik_own", [128, NOWN * 1024], BF16, D_OFF)
        y_t = alloc("y_t", [128, NOWN * 1024], BF16, D_OFF)
        s_sb = alloc("s_sb", [128, 8 * TOWN], BF16, D_OFF)
        v_own = alloc("v_own", [128, NOWN * 1024], BF16, E_OFF)
        wo_sb = alloc("wo_sb", [128, 8 * 1024], BF16, E_OFF)
        xo_buf = [alloc(f"xo{i}", [128, 1024], F32, E_OFF + i * 4096) for i in range(2)]
        t1_buf = [alloc(f"t1{i}", [128, 1024], F32, E_OFF + 8192 + i * 4096) for i in range(2)]
        g_buf = [alloc(f"gb{i}", [128, 1024], BF16, E_OFF + 16384 + i * 2048) for i in range(2)]
        w1buf = [alloc(f"w1b{i}", [128, 1024], BF16, E_OFF + 20480 + i * 2048) for i in range(2)]
        w2buf = [alloc(f"w2b{i}", [128, 4096], BF16, C_OFF + i * 8192) for i in range(2)]
        phi_q = alloc("phi_q", [128, 8 * TOWN], BF16, F_OFF)
        h_lo = alloc("h_lo", [128, 16 * 1024], BF16, F_OFF)
        wk_bf = alloc("wk_bf", [128, 8 * 1024], BF16, F_OFF)
        wv_bf = alloc("wv_bf", [128, 8 * 1024], BF16, F_OFF + 16384)
        g_xt = [alloc(f"g_xt{i}", [128, 1024], F32, G_OFF + i * 4096) for i in range(2)]
        g_xnt = [alloc(f"g_xnt{i}", [128, 1024], BF16, G_OFF + 8192 + i * 2048) for i in range(2)]
        g_xnf = [alloc(f"g_xnf{i}", [128, 1024], BF16, G_OFF + 12288 + i * 2048) for i in range(2)]
        g_e = [alloc(f"g_e{i}", [128, 1024], BF16, G_OFF + 16384 + i * 2048) for i in range(2)]
        g_pk = [alloc(f"g_pk{i}", [128, 1024], BF16, G_OFF + 20480 + i * 2048) for i in range(2)]
        g_pv = [alloc(f"g_pv{i}", [128, 1024], BF16, G_OFF + 24576 + i * 2048) for i in range(2)]
        g_r = [alloc(f"g_r{i}", [128, 1024], BF16, G_OFF + 28672 + i * 2048) for i in range(2)]
        wq_bf = alloc("wq_bf", [128, 8 * 1024], BF16, G_OFF)
        y_f = alloc("y_f", [128, 8 * TOWN], BF16, G_OFF)
        h_hi = alloc("h_hi", [128, 16 * 1024], BF16, G_OFF)
        s_loc = alloc("s_loc", [128, NOWN * 520], BF16, C_OFF)

        moff = [M_OFF]
        def malloc(name, shape, dt):
            esz = {F32: 4, BF16: 2}[dt]
            per_part = int(np.prod(shape[1:])) * esz
            off = moff[0]
            moff[0] = off + ((per_part + 31) // 32) * 32
            assert moff[0] <= SBUF_LIMIT, f"misc overflow at {name}"
            return nc.alloc_sbuf_tensor_at(name, list(shape), dt, offset=off)

        qw_buf = [malloc(f"qw{i}", [128, 1024], F32) for i in range(3)]
        qs_buf = [malloc(f"qs{i}", [128, 1024], BF16) for i in range(2)]
        qo_buf = [malloc(f"qo{i}", [128, 1024], BF16) for i in range(2)]
        s_run = malloc("s_run", [128, 520], F32)
        kf_buf = [malloc(f"kf{i}", [128, 8 * 128], BF16) for i in range(2)]
        am_buf = [malloc(f"am{i}", [128, 128], BF16) for i in range(3)]
        stat6 = [malloc(f"st6{i}", [128, 12], F32) for i in range(2)]
        mv_b = [malloc(f"mv{i}", [128, 2], F32) for i in range(2)]
        lnv_b = [malloc(f"lnv{i}", [128, 1], F32) for i in range(2)]
        rstd_b = [malloc(f"rstd{i}", [128, 1], F32) for i in range(2)]
        zc_b = [malloc(f"zc{i}", [128, 1], F32) for i in range(3)]
        zr_b = [malloc(f"zr{i}", [128, 1], F32) for i in range(3)]
        acc_q = malloc("acc_q", [128, 2], F32)
        accp = malloc("accp", [128, 1], F32)
        t_row = malloc("t_row", [1, 2], F32)
        tb_w = [malloc(f"tbw{i}", [128, 1], F32) for i in range(2)]
        tn_w = [malloc(f"tnw{i}", [128, 1], F32) for i in range(2)]
        b1x2 = malloc("b1x2", [128, 32], F32)
        sb_maskT = malloc("sb_maskT", [128, 128], F32)
        sb_cmask = malloc("sb_cmask", [128, NT], BF16)
        sb_onesb = malloc("sb_onesb", [128, 1], BF16)
        sb_onesf = malloc("sb_onesf", [128, 1], F32)
        sb_onesfr = malloc("sb_onesfr", [1, 128], F32)
        sb_idn = malloc("sb_idn", [128, 128], BF16)
        sb_b1 = malloc("sb_b1", [128, 32], F32)
        sb_b2 = malloc("sb_b2", [128, 8], F32)

        pb = [ctx.enter_context(nc.psum_tensor(f"pb{i}", [128, 512], F32)) for i in range(8)]

        def mksem(name):
            return Sem(ctx.enter_context(nc.semaphore(name)))

        def mksem2(name):
            return [mksem(name + "a"), mksem(name + "b")]

        sCONST = mksem("sCONST")
        sXT = mksem2("sXT")
        sW = mksem("sW")
        sWB = mksem2("sWB")
        sQ4 = [mksem(f"sQ4_{i}") for i in range(3)]
        sQ1 = mksem2("sQ1")
        sQ2 = mksem2("sQ2")
        sQS = mksem2("sQS")
        sTR = mksem2("sTR")
        sTRS = mksem2("sTRS")
        sKF = mksem2("sKF")
        sYT = mksem("sYT")
        sST = mksem2("sST")
        sXO = mksem2("sXO")
        sW1 = mksem2("sW1")
        sW2 = mksem2("sW2")
        sTAP = mksem("sTAP")
        sTAPB = mksem("sTAPB")
        sPE = mksem("sPE")
        sVE = mksem("sVE")
        sSE = mksem("sSE")
        sPO = mksem("sPO")

        B = Builder()
        op = B.op

        def ld(dst, src, waits=()):
            return op("sp", lambda e, d=dst, s=src: e.dma_start(out=d, in_=s),
                      waits=waits, inc=sCONST, inc_by=16)

        ld(sb_maskT[:, :], d_in["maskT"][:, :])
        ld(sb_cmask[:, :], d_in["cmask"][:, :])
        ld(sb_onesb[:, :], d_in["onesb"][:, :])
        ld(sb_onesf[:, :], d_in["onesf"][:, :])
        ld(sb_onesfr[:, :], d_in["onesfr"][:, :])
        ld(sb_idn[:, :], d_in["idn"][:, :])
        ld(sb_b1[:, :], d_in["b1m"][:, :])
        tok_consts = ld(sb_b2[:, :], d_in["b2m"][:, :])

        def wload(dst, name, waits=()):
            return op("ge", lambda e, d=dst, n=name: e.dma_start(
                out=d.rearrange("p (a d) -> p a d", a=8),
                in_=d_in[n].rearrange("(a p) d -> p a d", p=128)),
                waits=waits, inc=sW, inc_by=16)

        tok_wk = wload(wk_bf, "wkT")
        tok_wv = wload(wv_bf, "wvT")

        tok_b1x2 = op("ve", lambda e: e.tensor_scalar(b1x2[:, :], sb_b1[:, :], 2.0, None, ALU.mult),
                      waits=[tok_consts], inc=sVE)
        tok_srun0 = op("ve", lambda e: e.memset(s_run[:, :], 0.0), inc=sVE)
        tok_acc0 = op("ve", lambda e: e.memset(acc_q[:, :], 0.0), inc=sVE)

        q_tiles = []
        for r in range(8):
            for c in range(4):
                q_tiles.append((0, (r, c)))
        for r in range(32):
            q_tiles.append((1, (r,)))

        def q_src(m, idx):
            if m == 0:
                r, c = idx
                return d_in["w1T"][r * 128:(r + 1) * 128, c * 1024:(c + 1) * 1024]
            (r,) = idx
            return d_in["w2T"][r * 128:(r + 1) * 128, :]

        def q_dst(m, idx):
            if m == 0:
                r, c = idx
                return wq1_d[r * 128:(r + 1) * 128, c * 1024:(c + 1) * 1024]
            (r,) = idx
            return wq2_d[r * 128:(r + 1) * 128, :]

        p1_state = {"red": [None, None], "acc": tok_acc0}

        def emit_p1(i):
            m, idx = q_tiles[i]
            tk = op("ge", lambda e, i=i, m=m, idx=idx: e.dma_start(
                out=qw_buf[i % 2][:, :], in_=q_src(m, idx)),
                waits=[p1_state["red"][i % 2]], inc=sQ1[i % 2], inc_by=16)
            t1 = op("ve", lambda e, i=i: e.tensor_reduce(
                accp[:, :], qw_buf[i % 2][:, :], AX.X, ALU.add, apply_absolute_value=True),
                waits=[tk, p1_state["acc"]], inc=sVE)
            p1_state["acc"] = op("po", lambda e, m=m: e.tensor_tensor(
                acc_q[:, m:m + 1], accp[:, :], acc_q[:, m:m + 1], ALU.add),
                waits=[t1], inc=sPO)
            p1_state["red"][i % 2] = p1_state["acc"]

        def stop_after(phase, toks):
            if BUILD_UPTO == phase:
                op("sp", lambda e: None, waits=toks)
                blk = ctx.enter_context(nc.Block())
                blk.sync(lambda e: B.run(e, "sp"))
                blk.gpsimd(lambda e: B.run(e, {"ge", "po"}))
                blk.vector(lambda e: B.run(e, "ve"))
                blk.scalar(lambda e: B.run(e, "se"))
                blk.tensor(lambda e: B.run(e, "te"))
                return True
            return False

        if stop_after("loads", [tok_consts, tok_wk, tok_wv, tok_b1x2]):
            return nc

        tok_xt = [None] * NT
        tok_apply = [None] * NT
        tok_trp = [None] * NT
        tok_xnf = [None] * NT
        tok_proj = [None] * NT
        tok_phi = [None] * NT
        tok_vev = [None] * NT
        tok_kv = [None] * NT
        tok_sadd = [None] * NT

        def tile_refs(n):
            par = n % 2
            pset = pb[4 * par: 4 * par + 4]
            own = n >= OWN0
            oc = n - OWN0
            pdst = phik_own[:, oc * 1024:(oc + 1) * 1024] if own else g_pk[n % 2][:, :]
            vdst = v_own[:, oc * 1024:(oc + 1) * 1024] if own else g_pv[n % 2][:, :]
            return pset, own, oc, pdst, vdst

        def xnf_dst(n, dc):
            if n >= OWN0:
                oc = n - OWN0
                return xn_own[:, dc * TOWN + oc * CH: dc * TOWN + oc * CH + CH]
            return g_xnf[n % 2][:, dc * CH:(dc + 1) * CH]

        def emit_xt(n):
            tok_xt[n] = op("sp", lambda e, n=n: e.dma_start(
                out=g_xt[n % 2][:, :], in_=d_in["x_seq"][n * CH:(n + 1) * CH, :]),
                waits=[tok_apply[n - 2] if n >= 2 else None], inc=sXT[n % 2], inc_by=16)

        def emit_ln(n):
            tv1 = op("ve", lambda e, n=n: e.bn_stats(stat6[n % 2][:, 0:6], g_xt[n % 2][:, 0:512]),
                     waits=[tok_xt[n]], inc=sVE)
            tv2 = op("ve", lambda e, n=n: e.bn_stats(stat6[n % 2][:, 6:12], g_xt[n % 2][:, 512:1024]),
                     waits=[tv1], inc=sVE)
            tv3 = op("ve", lambda e, n=n: e.bn_aggr(mv_b[n % 2][:, :], stat6[n % 2][:, :]),
                     waits=[tv2], inc=sVE)
            tv4 = op("ve", lambda e, n=n: e.tensor_scalar(
                lnv_b[n % 2][:, :], mv_b[n % 2][:, 1:2], LNEPS, None, ALU.add),
                waits=[tv3], inc=sVE)
            ts1 = op("se", lambda e, n=n: e.activation(
                rstd_b[n % 2][:, :], lnv_b[n % 2][:, :], AF.Ln), waits=[tv4], inc=sSE)
            ts2 = op("se", lambda e, n=n: e.activation(
                rstd_b[n % 2][:, :], rstd_b[n % 2][:, :], AF.Exp, scale=-0.5),
                waits=[ts1], inc=sSE)
            tok_apply[n] = op("ve", lambda e, n=n: e.tensor_scalar(
                g_xnt[n % 2][:, :], g_xt[n % 2][:, :], mv_b[n % 2][:, 0:1], rstd_b[n % 2][:, 0:1],
                ALU.subtract, ALU.mult),
                waits=[ts2, tok_trp[n - 2] if n >= 2 else None], inc=sVE)

        def emit_tr(n):
            pset, own, oc, _, _ = tile_refs(n)
            w0 = [tok_apply[n], tok_consts,
                  tok_vev[n - 2] if n >= 2 else None]

            def trp(e, n=n, pset=pset):
                for dc in range(8):
                    bank = pset[2 + dc // 4]
                    ins = e.matmul(bank[:, (dc % 4) * 128:(dc % 4 + 1) * 128],
                                   g_xnt[n % 2][:, dc * CH:(dc + 1) * CH], sb_idn[:, :],
                                   start=True, stop=True, skip_group_check=True)
                return ins
            tok_trp[n] = op("te", trp, waits=w0, inc=sPE)
            last = None
            for dc in range(8):
                bank = pset[2 + dc // 4]
                last = op("se", lambda e, n=n, dc=dc, bank=bank: e.activation(
                    xnf_dst(n, dc), bank[:, (dc % 4) * 128:(dc % 4 + 1) * 128], AF.Copy),
                    waits=[tok_trp[n], tok_proj[n - 2] if n >= 2 else None], inc=sSE)
            tok_xnf[n] = last

        def emit_proj(n):
            pset, own, oc, _, _ = tile_refs(n)
            w0 = [tok_xnf[n], tok_wk, tok_wv]
            if n >= 2:
                w0 += [tok_phi[n - 2], tok_vev[n - 2], tok_sadd[n - 2]]

            def proj(e, n=n, own=own, oc=oc, pset=pset):
                for dc in range(8):
                    if own:
                        lhs = xn_own[:, dc * TOWN + oc * CH: dc * TOWN + oc * CH + CH]
                    else:
                        lhs = g_xnf[n % 2][:, dc * CH:(dc + 1) * CH]
                    st, sp = (dc == 0), (dc == 7)
                    e.matmul(pset[0][:, :], lhs, wk_bf[:, dc * 1024:dc * 1024 + 512],
                             start=st, stop=sp, skip_group_check=True)
                    e.matmul(pset[1][:, :], lhs, wk_bf[:, dc * 1024 + 512:(dc + 1) * 1024],
                             start=st, stop=sp, skip_group_check=True)
                    e.matmul(pset[2][:, :], lhs, wv_bf[:, dc * 1024:dc * 1024 + 512],
                             start=st, stop=sp, skip_group_check=True)
                    ins = e.matmul(pset[3][:, :], lhs, wv_bf[:, dc * 1024 + 512:(dc + 1) * 1024],
                                   start=st, stop=sp, skip_group_check=True)
                return ins
            tok_proj[n] = op("te", proj, waits=w0, inc=sPE)

        def emit_phiv(n):
            pset, own, oc, pdst, vdst = tile_refs(n)
            tvv = op("se", lambda e, pset=pset, d=vdst: e.activation(
                d[:, 0:512], pset[2][:, :], AF.Copy),
                waits=[tok_proj[n], tok_phi[n - 2] if n >= 2 else None], inc=sSE)
            tok_vev[n] = op("se", lambda e, pset=pset, d=vdst: e.activation(
                d[:, 512:1024], pset[3][:, :], AF.Copy), waits=[tvv], inc=sSE)
            te1 = op("se", lambda e, pset=pset, n=n: e.activation(
                g_e[n % 2][:, 0:512], pset[0][:, :], AF.Exp), waits=[tok_vev[n]], inc=sSE)
            te2 = op("se", lambda e, pset=pset, n=n: e.activation(
                g_e[n % 2][:, 512:1024], pset[1][:, :], AF.Exp), waits=[te1], inc=sSE)
            tm1 = op("ve", lambda e, pset=pset, n=n, d=pdst: e.tensor_scalar(
                d[:, 0:512], pset[0][:, :], 0.0, None, ALU.max),
                waits=[tok_proj[n], te1], inc=sVE)
            tx1 = op("ve", lambda e, n=n, d=pdst: e.scalar_tensor_tensor(
                d[:, 0:512], g_e[n % 2][:, 0:512], 1.0, d[:, 0:512], ALU.min, ALU.add),
                waits=[tm1], inc=sVE)
            tm2 = op("ve", lambda e, pset=pset, n=n, d=pdst: e.tensor_scalar(
                d[:, 512:1024], pset[1][:, :], 0.0, None, ALU.max),
                waits=[te2], inc=sVE)
            tok_phi[n] = op("ve", lambda e, n=n, d=pdst: e.scalar_tensor_tensor(
                d[:, 512:1024], g_e[n % 2][:, 512:1024], 1.0, d[:, 512:1024],
                ALU.min, ALU.add), waits=[tm2], inc=sVE)

        def emit_kv(m):
            pset, own, oc, pdst, vdst = tile_refs(m)

            def kvmm(e, m=m, pset=pset, pdst=pdst, vdst=vdst):
                for h in range(HEADS):
                    bank = pset[0] if h < 8 else pset[1]
                    rows = slice((h % 2) * 64, (h % 2) * 64 + 64)
                    blk_ = (h // 2) % 4
                    lhs = pdst[:, h * 64:(h + 1) * 64]
                    st = (h == 0 or h == 8)
                    e.matmul(bank[rows, blk_ * 65: blk_ * 65 + 64], lhs,
                             vdst[:, h * 64:(h + 1) * 64], start=st, stop=False,
                             skip_group_check=True)
                    ins = e.matmul(bank[rows, blk_ * 65 + 64: blk_ * 65 + 65], lhs,
                                   sb_cmask[:, m:m + 1], start=False, stop=(h == 15),
                                   skip_group_check=True)
                return ins
            tok_kv[m] = op("te", kvmm, waits=[tok_phi[m], tok_vev[m], tok_consts], inc=sPE)

            wv_ = [tok_kv[m], tok_srun0, tok_sadd[m - 1] if m > 0 else None]
            if own:
                tsl = op("ve", lambda e, oc=oc: e.tensor_copy(
                    s_loc[:, oc * 520:(oc + 1) * 520], s_run[:, :]),
                    waits=[tok_sadd[m - 1] if m > 0 else tok_srun0], inc=sVE)
                wv_.append(tsl)
            ta1 = op("ve", lambda e, pset=pset: e.scalar_tensor_tensor(
                s_run[:, 0:260], pset[0][:, 0:260], 1.0, s_run[:, 0:260], ALU.mult, ALU.add),
                waits=wv_, inc=sVE)
            tok_sadd[m] = op("ve", lambda e, pset=pset: e.scalar_tensor_tensor(
                s_run[:, 260:520], pset[1][:, 0:260], 1.0, s_run[:, 260:520], ALU.mult, ALU.add),
                waits=[ta1], inc=sVE)

        emit_xt(0)
        emit_ln(0)
        for slot in range(NT + 2):
            if slot + 1 < NT:
                emit_xt(slot + 1)
            if slot < NT:
                emit_tr(slot)
            if slot + 1 < NT:
                emit_ln(slot + 1)
            if 1 <= slot <= NT:
                emit_proj(slot - 1)
            if slot >= 2:
                emit_kv(slot - 2)
            if 1 <= slot <= NT:
                emit_phiv(slot - 1)
            if slot < NT:
                emit_p1(slot)

        tok_scan_pe = tok_kv[NT - 1]
        tok_scan_done = tok_sadd[NT - 1]
        tok_p1_done = p1_state["acc"]

        if stop_after("scan", [tok_scan_pe, tok_scan_done, tok_phi[NT - 1], tok_vev[NT - 1],
                               tok_xnf[NT - 1], tok_p1_done]):
            return nc

        tok_tot = op("te", lambda e: e.matmul(pb[0][0:1, 0:2], sb_onesf[:, :], acc_q[:, :],
                                              start=True, stop=True, skip_group_check=True),
                     waits=[tok_p1_done, tok_scan_pe, tok_phi[NT - 1], tok_vev[NT - 1],
                            tok_sadd[NT - 1]], inc=sPE)
        kq = 0.5 / (DFF * D)
        tok_trow = op("ve", lambda e: e.tensor_scalar(
            t_row[0:1, :], pb[0][0:1, 0:2], kq, 0.5 * QEPS, ALU.mult, ALU.add),
            waits=[tok_tot], inc=sVE)
        if "tvals" in taps:
            op("sp", lambda e: e.dma_start(out=taps["tvals"][:, :], in_=t_row[0:1, :]),
               waits=[tok_trow], inc=sTAPB, inc_by=16)
        tok_bc = [None, None]
        for m in range(2):
            tt = op("te", lambda e, m=m: e.matmul(
                pb[1][:, m:m + 1], sb_onesfr[0:1, :], t_row[0:1, m:m + 1],
                start=(m == 0), stop=(m == 1), skip_group_check=True),
                waits=[tok_trow], inc=sPE)
            t1 = op("ve", lambda e, m=m: e.tensor_copy(tb_w[m][:, :], pb[1][:, m:m + 1]),
                    waits=[tt], inc=sVE)
            tok_bc[m] = op("ve", lambda e, m=m: e.tensor_scalar(
                tn_w[m][:, :], tb_w[m][:, :], -1.0, None, ALU.mult), waits=[t1], inc=sVE)

        if stop_after("boundary", [tok_bc[0], tok_bc[1]]):
            return nc

        p2_state = {"sign": [None, None], "signq": [None] * 3, "store": [None, None],
                    "pend": [None, None]}

        def flush_p2_store(par):
            if p2_state["pend"][par] is None:
                return
            si, tadd = p2_state["pend"][par]
            p2_state["pend"][par] = None
            m, idx = q_tiles[si]
            p2_state["store"][par] = op("ge", lambda e, si=si, m=m, idx=idx: e.dma_start(
                out=q_dst(m, idx), in_=qo_buf[si % 2][:, :]), waits=[tadd],
                inc=sQS[par], inc_by=16)
            if si == 31:
                p2_state["w1a"] = p2_state["store"][0]
                p2_state["w1b"] = p2_state["store"][1]

        def emit_p2(i, addeng="ve"):
            m, idx = q_tiles[i]
            tk = op("ge", lambda e, i=i, m=m, idx=idx: e.dma_start(
                out=qw_buf[i % 3][:, :], in_=q_src(m, idx)),
                waits=[tok_bc[1], p2_state["signq"][i % 3]], inc=sQ4[i % 3], inc_by=16)
            flush_p2_store(i % 2)
            tsa = op("se", lambda e, i=i, m=m: e.activation(
                qs_buf[i % 2][:, :], qw_buf[i % 3][:, :], AF.Sign, bias=tn_w[m][:, 0:1]),
                waits=[tk, p2_state["store"][i % 2],
                       p2_state.get("tap0") if i == 2 else None], inc=sSE)
            tsb = op("se", lambda e, i=i, m=m: e.activation(
                qo_buf[i % 2][:, :], qw_buf[i % 3][:, :], AF.Sign, bias=tb_w[m][:, 0:1]),
                waits=[tsa], inc=sSE)
            p2_state["sign"][i % 2] = tsb
            p2_state["signq"][i % 3] = tsb
            if addeng == "po":
                tadd = op("po", lambda e, i=i: e.tensor_tensor(
                    qo_buf[i % 2][:, :], qs_buf[i % 2][:, :], qo_buf[i % 2][:, :],
                    ALU.add), waits=[tsb], inc=sPO)
            else:
                tadd = op("ve", lambda e, i=i: e.scalar_tensor_tensor(
                    qo_buf[i % 2][:, :], qs_buf[i % 2][:, :], 1.0, qo_buf[i % 2][:, :],
                    ALU.mult, ALU.add), waits=[tsb], inc=sVE)
            if "wq1" in taps and i == 0:
                p2_state["tap0"] = op(
                    "sp", lambda e: e.dma_start(out=taps["wq1"][:, :], in_=qo_buf[0][:, :]),
                    waits=[tadd], inc=sTAPB, inc_by=16)
            p2_state["pend"][i % 2] = (i, tadd)

        tok_wqb = [None] * 8
        for dcb in range(8):
            wts = [tok_scan_pe, tok_scan_done, tok_phi[NT - 1], tok_vev[NT - 1],
                   tok_xnf[NT - 1], tok_apply[NT - 1]] if dcb == 0 else []
            if dcb >= 2:
                wts.append(tok_wqb[dcb - 2])
            tok_wqb[dcb] = op("ge", lambda e, dcb=dcb: e.dma_start(
                out=wq_bf[:, dcb * 1024:(dcb + 1) * 1024].rearrange("p (a d) -> p a d", a=8),
                in_=d_in["wqT"][:, dcb * 128:(dcb + 1) * 128].rearrange("(a p) d -> p a d", p=128)),
                waits=wts, inc=sWB[dcb % 2], inc_by=16)

        tok_qphi = [None] * 8
        tok_qproj = [None] * 8
        qphi_par = [None, None]
        for dc in range(8):
            par = dc % 2
            pset = pb[4 * par: 4 * par + 4]
            w0 = [tok_wqb[dc], tok_xnf[NT - 1], tok_bc[1], tok_tot]
            if dc >= 2:
                w0 += [qphi_par[0], qphi_par[1]]

            def qproj(e, dc=dc, pset=pset):
                for di in range(8):
                    lhs = wq_bf[:, dc * 1024 + di * 128: dc * 1024 + (di + 1) * 128]
                    st, sp = (di == 0), (di == 7)
                    for ts in range(4):
                        ins = e.matmul(pset[ts][:, :], lhs,
                                       xn_own[:, di * TOWN + ts * 512: di * TOWN + (ts + 1) * 512],
                                       start=st, stop=sp, skip_group_check=True)
                return ins
            tok_qproj[dc] = op("te", qproj, waits=w0, inc=sPE)
            for ts in range(4):
                dst = phi_q[:, dc * TOWN + ts * 512: dc * TOWN + (ts + 1) * 512]
                bi = ts % 2
                te1 = op("se", lambda e, pset=pset, ts=ts, bi=bi: e.activation(
                    g_e[bi][:, 0:512], pset[ts][:, :], AF.Exp),
                    waits=[tok_qproj[dc], qphi_par[bi]], inc=sSE)
                tm1 = op("ve", lambda e, pset=pset, ts=ts, d=dst: e.tensor_scalar(
                    d, pset[ts][:, :], 0.0, None, ALU.max),
                    waits=[tok_qproj[dc], te1], inc=sVE)
                qphi_par[bi] = op("ve", lambda e, bi=bi, d=dst: e.scalar_tensor_tensor(
                    d, g_e[bi][:, 0:512], 1.0, d, ALU.min, ALU.add),
                    waits=[tm1], inc=sVE)
            tok_qphi[dc] = qphi_par[(len(range(4)) - 1) % 2]
            emit_p2(dc)
        tok_qdone = tok_qphi[7]

        tap_gate = None
        for tname, buf, tokw in [("xn", xn_own, tok_xnf[NT - 1]),
                                 ("phik", phik_own, tok_phi[NT - 1]),
                                 ("v", v_own, tok_vev[NT - 1]),
                                 ("sloc", s_loc, tok_scan_done),
                                 ("phiq", phi_q, tok_qdone)]:
            if tname in taps:
                tap_gate = op("sp", lambda e, b=buf, t=tname: e.dma_start(
                    out=taps[t][:, :], in_=b[:, :]), waits=[tokw], inc=sTAP, inc_by=16)

        if stop_after("qproj", [tok_qdone]):
            return nc

        tok_attnT = {}
        tok_am = {}
        tok_intra = {}
        tok_y = {}
        tok_kfT = [None] * NOWN
        tok_ytr = [None] * NOWN

        units = [(c, h) for c in range(NOWN) for h in range(HEADS)]

        def unit_refs(c, h):
            u = c * HEADS + h
            pa = [pb[0], pb[1], pb[4]][u % 3]
            py = [pb[2], pb[3], pb[5]][u % 3]
            rows = slice((h % 2) * 64, (h % 2) * 64 + 64)
            qsl = phi_q[rows, (h // 2) * TOWN + c * CH: (h // 2) * TOWN + (c + 1) * CH]
            ksl = kf_buf[c % 2][rows, (h // 2) * 128:(h // 2) * 128 + 128]
            vsl = v_own[:, c * 1024 + h * 64: c * 1024 + (h + 1) * 64]
            ssl = s_loc[rows, c * 520 + (h // 2) * 65: c * 520 + (h // 2) * 65 + 65]
            return u, pa, py, qsl, ksl, vsl, ssl

        def prev_tok(dct, c, h, k=2):
            u = c * HEADS + h - k
            if u < 0:
                return None
            return dct.get((u // HEADS, u % HEADS))

        def emit_head(c, h):
            u, pa, py, qsl, ksl, vsl, ssl = unit_refs(c, h)
            tok_attnT[(c, h)] = op("te", lambda e, pa=pa, ksl=ksl, qsl=qsl: e.matmul(
                pa[:, 0:128], ksl, qsl, start=True, stop=True, skip_group_check=True),
                waits=[tok_kfT[c], tok_qdone, prev_tok(tok_am, c, h, 3)], inc=sPE)
            tok_am[(c, h)] = op("ve", lambda e, u=u, pa=pa: e.tensor_tensor(
                am_buf[u % 3][:, :], pa[:, 0:128], sb_maskT[:, :], ALU.mult),
                waits=[tok_attnT[(c, h)], prev_tok(tok_intra, c, h, 3)], inc=sVE)

        def emit_tail(c, h):
            u, pa, py, qsl, ksl, vsl, ssl = unit_refs(c, h)

            def intra(e, u=u, py=py, vsl=vsl, qsl=qsl, ssl=ssl):
                e.matmul(py[:, 0:64], am_buf[u % 3][:, :], vsl, start=True, stop=False,
                         skip_group_check=True)
                e.matmul(py[:, 64:65], am_buf[u % 3][:, :], sb_onesb[:, :], start=False,
                         stop=False, skip_group_check=True)
                return e.matmul(py[:, 0:65], qsl, ssl, start=False, stop=True,
                                skip_group_check=True)
            tok_intra[(c, h)] = op("te", intra,
                                   waits=[tok_am[(c, h)], tok_scan_done,
                                          prev_tok(tok_y, c, h, 3)], inc=sPE)
            tz1 = op("ve", lambda e, u=u, py=py: e.tensor_scalar(
                zc_b[u % 3][:, :], py[:, 64:65], 1e-6, None, ALU.max),
                waits=[tok_intra[(c, h)], prev_tok(tok_y, c, h, 3)], inc=sVE)
            tz2 = op("ve", lambda e, u=u: e.reciprocal(zr_b[u % 3][:, :], zc_b[u % 3][:, :]),
                     waits=[tz1], inc=sVE)
            tok_y[(c, h)] = op("se", lambda e, u=u, py=py, c=c, h=h: e.activation(
                y_t[:, c * 1024 + h * 64: c * 1024 + (h + 1) * 64], py[:, 0:64],
                AF.Copy, scale=zr_b[u % 3][:, 0:1]),
                waits=[tz2, tok_kfT[c], tap_gate], inc=sSE)
            if h == 15:
                pair = [pb[6], pb[7]]
                w0 = [tok_y[(c, 15)], tok_qdone, trp_state["evac"]]

                def ytr(e, c=c, pair=pair):
                    for ec in range(8):
                        bank = pair[ec // 4]
                        ins = e.matmul(bank[:, (ec % 4) * 128:(ec % 4 + 1) * 128],
                                       y_t[:, c * 1024 + ec * 128: c * 1024 + (ec + 1) * 128],
                                       sb_idn[:, :], start=True, stop=True,
                                       skip_group_check=True)
                    return ins
                tpe_y = op("te", ytr, waits=w0, inc=sPE)
                last = None
                for ec in range(8):
                    bank = pair[ec // 4]
                    last = op("se", lambda e, c=c, ec=ec, bank=bank: e.activation(
                        y_f[:, ec * TOWN + c * CH: ec * TOWN + (c + 1) * CH],
                        bank[:, (ec % 4) * 128:(ec % 4 + 1) * 128], AF.Copy),
                        waits=[tpe_y, last], inc=sSE)
                tok_ytr[c] = last
                trp_state["evac"] = last
                emit_p2(8 + (3 * c + 1) // 2, "po")
                if c % 2 == 0:
                    emit_p2(9 + (3 * c + 1) // 2, "po")

        trp_state = {"evac": tok_qdone}
        for i, (c, h) in enumerate(units):
            if h == 0:
                pair = [pb[6], pb[7]]
                w0 = [tok_phi[OWN0 + c], tok_qdone,
                      tok_attnT.get((c - 2, 15)) if c >= 2 else None,
                      trp_state["evac"]]

                def kft(e, c=c, pair=pair):
                    for fc in range(8):
                        bank = pair[fc // 4]
                        ins = e.matmul(bank[:, (fc % 4) * 128:(fc % 4 + 1) * 128],
                                       phik_own[:, c * 1024 + fc * 128: c * 1024 + (fc + 1) * 128],
                                       sb_idn[:, :], start=True, stop=True,
                                       skip_group_check=True)
                    return ins
                tpe_k = op("te", kft, waits=w0, inc=sPE)
                last = None
                for fc in range(8):
                    bank = pair[fc // 4]
                    if fc < 4:
                        last = op("ve", lambda e, c=c, fc=fc, bank=bank: e.tensor_copy(
                            kf_buf[c % 2][:, fc * 128:(fc + 1) * 128],
                            bank[:, (fc % 4) * 128:(fc % 4 + 1) * 128]),
                            waits=[tpe_k, last], inc=sVE)
                    else:
                        last = op("se", lambda e, c=c, fc=fc, bank=bank: e.activation(
                            kf_buf[c % 2][:, fc * 128:(fc + 1) * 128],
                            bank[:, (fc % 4) * 128:(fc % 4 + 1) * 128], AF.Copy),
                            waits=[tpe_k, last], inc=sSE)
                tok_kfT[c] = last
                trp_state["evac"] = last
            emit_head(c, h)
            if i > 1:
                emit_tail(*units[i - 2])
        emit_tail(*units[-2])
        emit_tail(*units[-1])

        tap_gate2 = None
        if "yt" in taps:
            tap_gate2 = op("sp", lambda e: e.dma_start(out=taps["yt"][:, :], in_=y_t[:, :]),
                           waits=[tok_y[(NOWN - 1, 15)]], inc=sTAP, inc_by=16)
        tap_gate3 = None
        if "yf" in taps:
            tap_gate3 = op("sp", lambda e: e.dma_start(out=taps["yf"][:, :], in_=y_f[:, :]),
                           waits=[tok_ytr[NOWN - 1]], inc=sTAP, inc_by=16)

        if stop_after("phb", [tok_y[(NOWN - 1, 15)], tok_ytr[NOWN - 1]]):
            return nc

        tok_wob = [None] * 8
        for dcb in range(8):
            wts = [tok_intra[(NOWN - 1, 15)], tok_y[(NOWN - 1, 15)], tap_gate,
                   tok_wqb[6 + dcb % 2] if dcb < 2 else tok_wob[dcb - 2]]
            tok_wob[dcb] = op("ge", lambda e, dcb=dcb: e.dma_start(
                out=wo_sb[:, dcb * 1024:(dcb + 1) * 1024].rearrange("p (a d) -> p a d", a=8),
                in_=d_in["woT"][:, dcb * 128:(dcb + 1) * 128].rearrange("(a p) d -> p a d", p=128)),
                waits=wts, inc=sWB[dcb % 2], inc_by=16)

        tok_wos = [None] * 8
        for dc in range(8):
            par = dc % 2
            pset = pb[4 * par: 4 * par + 4]
            w0 = [tok_wob[dc], tok_ytr[NOWN - 1],
                  tok_y[(NOWN - 1, 15)], tok_am[(NOWN - 1, 15)]]
            if dc >= 2:
                w0.append(tok_wos[dc - 2])

            def wos(e, dc=dc, pset=pset):
                for di in range(8):
                    lhs = wo_sb[:, dc * 1024 + di * 128: dc * 1024 + (di + 1) * 128]
                    st, sp = (di == 0), (di == 7)
                    for ts in range(4):
                        ins = e.matmul(pset[ts][:, :], lhs,
                                       y_f[:, di * TOWN + ts * 512: di * TOWN + (ts + 1) * 512],
                                       start=st, stop=sp, skip_group_check=True)
                return ins
            tpe = op("te", wos, waits=w0, inc=sPE)
            lastv = None
            for ts in range(4):
                lastv = op("ve", lambda e, dc=dc, ts=ts, pset=pset: e.tensor_copy(
                    s_sb[:, dc * TOWN + ts * 512: dc * TOWN + (ts + 1) * 512], pset[ts][:, :]),
                    waits=[tpe, lastv, tap_gate2], inc=sVE)
            tok_wos[dc] = lastv
            emit_p2(32 + dc)
        tok_s = tok_wos[7]
        tok_p2a = p2_state["store"][0]
        tok_p2b = p2_state["store"][1]

        tap_gate4 = None
        if "s" in taps:
            tap_gate4 = op("sp", lambda e: e.dma_start(out=taps["s"][:, :], in_=s_sb[:, :]),
                           waits=[tok_s], inc=sTAP, inc_by=16)
        tap_gate5 = None

        if stop_after("wo", [tok_s]):
            return nc

        tok_hev = {}
        tok_gev = {}
        tok_comb = {}
        tok_out = {}
        tok_xo = {}
        tok_w1use = [None, None]
        tok_w2use = [None, None]
        tok_w1l = [None, None]
        tok_w2l = [None, None]

        def gprev(dct, tb, j, k=2, per=8):
            g = tb * per + j - k
            if g < 0:
                return None
            return dct.get((g // per, g % per))

        for tb in range(2):
            for fc in range(32):
                i = fc % 2
                gidx = tb * 32 + fc
                tok_w1l[i] = op("sp", lambda e, fc=fc, i=i: e.dma_start(
                    out=w1buf[i].rearrange("p (a f) -> p a f", a=8),
                    in_=wq1_d[:, fc * 128:(fc + 1) * 128].rearrange("(a p) f -> p a f", p=128)),
                    waits=[p2_state["w1a"], p2_state["w1b"], tok_s, tok_w1use[i]],
                    inc=sW1[i], inc_by=16)
                m3 = fc % 3
                hps = [pb[0 + 2 * m3], pb[1 + 2 * m3]]
                w0 = [tok_w1l[i], tok_s, tap_gate3, tap_gate4]
                pg = gidx - 3
                if pg >= 0:
                    w0.append(tok_hev.get((pg // 32, pg % 32)))
                if tb == 1 and fc < 3:
                    w0 += [tok_gev.get((0, 7)), tok_comb.get((0, 7)), tap_gate5]

                def hmm(e, tb=tb, i=i, hps=hps):
                    for di in range(8):
                        lhs = w1buf[i][:, di * 128:(di + 1) * 128]
                        st, sp = (di == 0), (di == 7)
                        e.matmul(hps[0][:, :], lhs,
                                 xn_own[:, di * TOWN + tb * 1024: di * TOWN + tb * 1024 + 512],
                                 start=st, stop=sp, skip_group_check=True)
                        ins = e.matmul(hps[1][:, :], lhs,
                                       xn_own[:, di * TOWN + tb * 1024 + 512: di * TOWN + (tb + 1) * 1024],
                                       start=st, stop=sp, skip_group_check=True)
                    return ins
                tpe = op("te", hmm, waits=w0, inc=sPE)
                tok_w1use[i] = tpe
                hdst = h_lo if fc < 16 else h_hi
                hoff = (fc % 16) * 1024
                t1 = op("se", lambda e, hps=hps, d=hdst, o=hoff, fc=fc: e.activation(
                    d[:, o:o + 512], hps[0][:, :], AF.Relu, bias=b1x2[:, fc:fc + 1]),
                    waits=[tpe, tok_b1x2], inc=sSE)
                tok_hev[(tb, fc)] = op("se", lambda e, hps=hps, d=hdst, o=hoff, fc=fc: e.activation(
                    d[:, o + 512:o + 1024], hps[1][:, :], AF.Relu, bias=b1x2[:, fc:fc + 1]),
                    waits=[t1], inc=sSE)
                if tb == 0 and fc % 4 < 3 and 40 + (fc - fc // 4) < 64:
                    emit_p2(40 + (fc - fc // 4))

            if "h0" in taps and tb == 0:
                tap_gate5 = op("sp", lambda e: e.dma_start(
                    out=taps["h0"][:, :], in_=h_lo[:, 0:1024]),
                    waits=[tok_hev[(0, 0)]], inc=sTAP, inc_by=16)

            if tb == 0:
                flush_p2_store(0)
                flush_p2_store(1)
            for dc in range(8):
                i = dc % 2
                m3 = dc % 3
                gps = [pb[0 + 2 * m3], pb[1 + 2 * m3]]
                tok_w2l[i] = op("sp", lambda e, dc=dc, i=i: e.dma_start(
                    out=w2buf[i].rearrange("p (a f) -> p a f", a=32),
                    in_=wq2_d[:, dc * 128:(dc + 1) * 128]
                    .rearrange("(a p) f -> p a f", p=128)),
                    waits=[p2_state["store"][0], p2_state["store"][1], tok_s,
                           tok_w2use[i]], inc=sW2[i], inc_by=16)
                w0 = [tok_w2l[i], tok_hev[(tb, 31)], gprev(tok_gev, tb, dc, 3)]

                def gmm(e, i=i, gps=gps):
                    for fc in range(32):
                        lhs = w2buf[i][:, fc * 128:(fc + 1) * 128]
                        hsrc = h_lo if fc < 16 else h_hi
                        hoff = (fc % 16) * 1024
                        st, sp = (fc == 0), (fc == 31)
                        e.matmul(gps[0][:, :], lhs, hsrc[:, hoff:hoff + 512],
                                 start=st, stop=sp, skip_group_check=True)
                        ins = e.matmul(gps[1][:, :], lhs, hsrc[:, hoff + 512:hoff + 1024],
                                       start=st, stop=sp, skip_group_check=True)
                    return ins
                tpe = op("te", gmm, waits=w0, inc=sPE)
                tok_w2use[i] = tpe

                tg1 = op("se", lambda e, gps=gps, i=i, dc=dc: e.activation(
                    g_buf[i][:, 0:512], gps[0][:, :], AF.Silu, bias=sb_b2[:, dc:dc + 1], scale=0.25),
                    waits=[tpe, gprev(tok_comb, tb, dc)], inc=sSE)
                tok_gev[(tb, dc)] = op("se", lambda e, gps=gps, i=i, dc=dc: e.activation(
                    g_buf[i][:, 512:1024], gps[1][:, :], AF.Silu, bias=sb_b2[:, dc:dc + 1],
                    scale=0.25), waits=[tg1], inc=sSE)

                tok_xo[(tb, dc)] = op("sp", lambda e, tb=tb, dc=dc, i=i: e.dma_start(
                    out=xo_buf[i][:, :],
                    in_=d_in["x_ownT"][dc * 128:(dc + 1) * 128, tb * 1024:(tb + 1) * 1024]),
                    waits=[tok_s, gprev(tok_comb, tb, dc)], inc=sXO[i], inc_by=16)

                tm = op("ve", lambda e, tb=tb, dc=dc, i=i: e.tensor_tensor(
                    t1_buf[i][:, :], g_buf[i][:, :],
                    s_sb[:, dc * TOWN + tb * 1024: dc * TOWN + (tb + 1) * 1024], ALU.mult),
                    waits=[tok_gev[(tb, dc)], tok_s, gprev(tok_comb, tb, dc),
                           gprev(tok_out, tb, dc)], inc=sVE)
                tok_comb[(tb, dc)] = op("ve", lambda e, i=i: e.scalar_tensor_tensor(
                    t1_buf[i][:, :], xo_buf[i][:, :], 1.0, t1_buf[i][:, :], ALU.mult, ALU.add),
                    waits=[tm, tok_xo[(tb, dc)]], inc=sVE)
                tok_out[(tb, dc)] = op("sp", lambda e, tb=tb, dc=dc, i=i: e.dma_start(
                    out=out_f[dc * 128:(dc + 1) * 128, tb * 1024:(tb + 1) * 1024],
                    in_=t1_buf[i][:, :]),
                    waits=[tok_comb[(tb, dc)]], inc=sST[i], inc_by=16)

        op("sp", lambda e: None, waits=[sST[0].tok(), sST[1].tok()])

        blk = ctx.enter_context(nc.Block())
        blk.sync(lambda e: B.run(e, "sp"))
        blk.gpsimd(lambda e: B.run(e, {"ge", "po"}))
        blk.vector(lambda e: B.run(e, "ve"))
        blk.scalar(lambda e: B.run(e, "se"))
        blk.tensor(lambda e: B.run(e, "te"))

    return nc


_cached = {}


def kernel(L0_attn_x, L0_attn_W_q, L0_attn_W_k, L0_attn_W_v, L0_attn_W_o,
           L0_ffn_W1, L0_ffn_b1, L0_ffn_W2, L0_ffn_b2):
    x = np.asarray(L0_attn_x, dtype=np.float32)
    if "nc" not in _cached:
        _cached["nc"] = build_nc()
    nc = _cached["nc"]

    wqT = np.ascontiguousarray(np.asarray(L0_attn_W_q, np.float32).T)
    wkT = np.ascontiguousarray(np.asarray(L0_attn_W_k, np.float32).T)
    wvT = np.ascontiguousarray(np.asarray(L0_attn_W_v, np.float32).T)
    woT = np.ascontiguousarray(np.asarray(L0_attn_W_o, np.float32).T)
    w1T = np.ascontiguousarray(np.asarray(L0_ffn_W1, np.float32).T)
    w2T = np.ascontiguousarray(np.asarray(L0_ffn_W2, np.float32).T)
    b1m = np.ascontiguousarray(np.asarray(L0_ffn_b1, np.float32).reshape(32, 128).T)
    b2m = np.ascontiguousarray(np.asarray(L0_ffn_b2, np.float32).reshape(8, 128).T)
    maskT = np.ascontiguousarray(np.triu(np.ones((128, 128), np.float32)))
    onesb = np.ones((128, 1), bfloat16)
    onesf = np.ones((128, 1), np.float32)
    onesfr = np.ones((1, 128), np.float32)
    idn = np.eye(128, dtype=bfloat16)

    in_maps = []
    for c in range(8):
        b, j = c // 4, c % 4
        npad = TFULL - TOWN - j * TOWN
        xs = np.zeros((TFULL, D), np.float32)
        if j > 0:
            xs[npad:TFULL - TOWN] = x[b, :j * TOWN]
        xs[TFULL - TOWN:] = x[b, j * TOWN:(j + 1) * TOWN]
        x_ownT = np.ascontiguousarray(x[b, j * TOWN:(j + 1) * TOWN].T)
        cm = np.zeros((128, NT), bfloat16)
        cm[:, npad // CH:] = 1.0
        in_maps.append({
            "x_seq": xs, "x_ownT": x_ownT,
            "wqT": wqT, "wkT": wkT, "wvT": wvT, "woT": woT,
            "w1T": w1T, "w2T": w2T, "b1m": b1m, "b2m": b2m,
            "maskT": maskT, "cmask": cm, "onesb": onesb, "onesf": onesf, "onesfr": onesfr,
            "idn": idn,
        })

    res = run_bass_kernel_spmd(nc, in_maps, core_ids=list(range(8)))
    out = np.empty((2, TFULL, D), np.float32)
    for c in range(8):
        b, j = c // 4, c % 4
        out[b, j * TOWN:(j + 1) * TOWN] = np.asarray(res.results[c]["out_f"]).T
    if DEBUG_TAPS:
        kernel.last_results = res
    return out


# version 20 (final)
# speedup vs baseline: 1.2118x; 1.2118x over previous
import numpy as np
import ml_dtypes
from contextlib import ExitStack
import concourse.bass as bass
import concourse.mybir as mybir
from concourse.bass_utils import run_bass_kernel_spmd

F32 = mybir.dt.float32
BF16 = mybir.dt.bfloat16
AF = mybir.ActivationFunctionType
ALU = mybir.AluOpType
AX = mybir.AxisListType

D = 1024
HEADS = 16
TFULL = 8192
TOWN = 2048
CH = 128
NT = TFULL // CH
OWN0 = (TFULL - TOWN) // CH
NOWN = TOWN // CH
DFF = 4096
QEPS = 1e-5
LNEPS = 1e-5
BASE = 16512
SBUF_LIMIT = 229344

DEBUG_TAPS = []
SCAN_LEVEL = 5
PHI_MODE = 2
BUILD_UPTO = None
bfloat16 = ml_dtypes.bfloat16


class Sem:
    def __init__(self, h):
        self.h = h
        self.n = 0

    def tok(self):
        return (self, self.n)


class Builder:
    def __init__(self):
        self.plan = []

    def op(self, eng, fn, waits=(), inc=None, inc_by=1):
        merged = {}
        for t in waits:
            if t is None:
                continue
            s, v = t
            if v > 0 and merged.get(id(s), (None, 0))[1] < v:
                merged[id(s)] = (s, v)
        self.plan.append((eng, fn, list(merged.values()), inc, inc_by))
        if inc is not None:
            inc.n += inc_by
            return inc.tok()
        return None

    def run(self, eng_obj, names):
        if isinstance(names, str):
            names = {names}
        waited = {}
        for (e, fn, waits, inc, by) in self.plan:
            if e not in names:
                continue
            for (s, v) in waits:
                if waited.get(id(s), 0) < v:
                    eng_obj.wait_ge(s.h, v)
                    waited[id(s)] = v
            ins = fn(eng_obj)
            if inc is not None:
                ins.then_inc(inc.h, by)


def build_nc():
    nc = bass.Bass("TRN2", target_bir_lowering=False)

    d_in = {}
    for name, shape, dt in [
        ("x_seq", [TFULL, D], F32), ("x_ownT", [D, TOWN], F32),
        ("wqT", [D, D], F32), ("wkT", [D, D], F32), ("wvT", [D, D], F32), ("woT", [D, D], F32),
        ("w1T", [D, DFF], F32), ("w2T", [DFF, D], F32),
        ("b1m", [128, 32], F32), ("b2m", [128, 8], F32),
        ("maskT", [128, 128], F32), ("cmask", [128, NT], BF16),
        ("onesb", [128, 1], BF16), ("onesf", [128, 1], F32), ("onesfr", [1, 128], F32),
        ("idn", [128, 128], BF16),
    ]:
        d_in[name] = nc.dram_tensor(name, shape, dt, kind="ExternalInput").ap()
    out_f = nc.dram_tensor("out_f", [D, TOWN], F32, kind="ExternalOutput").ap()
    wq1_d = nc.dram_tensor("wq1_d", [D, DFF], BF16, kind="Internal").ap()
    wq2_d = nc.dram_tensor("wq2_d", [DFF, D], BF16, kind="Internal").ap()

    taps = {}
    for name, shape, dt in [
        ("xn", [128, 8 * TOWN], BF16), ("phik", [128, NOWN * 1024], BF16),
        ("v", [128, NOWN * 1024], BF16), ("sloc", [128, NOWN * 520], BF16),
        ("phiq", [128, 8 * TOWN], BF16), ("yt", [128, NOWN * 1024], BF16),
        ("yf", [128, 8 * TOWN], BF16), ("s", [128, 8 * TOWN], BF16),
        ("tvals", [1, 2], F32), ("wq1", [128, 1024], BF16), ("h0", [128, 1024], BF16),
    ]:
        if name in DEBUG_TAPS:
            taps[name] = nc.dram_tensor("tap_" + name, shape, dt, kind="ExternalOutput").ap()

    ctx = ExitStack()
    with ctx:
        def alloc(name, shape, dt, off):
            esz = {F32: 4, BF16: 2}[dt]
            per_part = int(np.prod(shape[1:])) * esz
            assert off >= BASE and off + per_part <= SBUF_LIMIT, f"{name}: {off}+{per_part}"
            return nc.alloc_sbuf_tensor_at(name, list(shape), dt, offset=off)

        A_OFF = BASE
        D_OFF = BASE + 32768
        E_OFF = BASE + 65536
        F_OFF = BASE + 98304
        G_OFF = BASE + 131072
        C_OFF = BASE + 163840
        M_OFF = BASE + 180480

        xn_own = alloc("xn_own", [128, 8 * TOWN], BF16, A_OFF)
        phik_own = alloc("phik_own", [128, NOWN * 1024], BF16, D_OFF)
        y_t = alloc("y_t", [128, NOWN * 1024], BF16, D_OFF)
        s_sb = alloc("s_sb", [128, 8 * TOWN], BF16, D_OFF)
        v_own = alloc("v_own", [128, NOWN * 1024], BF16, E_OFF)
        wo_sb = alloc("wo_sb", [128, 8 * 1024], BF16, E_OFF)
        xo_buf = [alloc(f"xo{i}", [128, 1024], F32, E_OFF + i * 4096) for i in range(2)]
        t1_buf = [alloc(f"t1{i}", [128, 1024], F32, E_OFF + 8192 + i * 4096) for i in range(2)]
        g_buf = [alloc(f"gb{i}", [128, 1024], BF16, E_OFF + 16384 + i * 2048) for i in range(2)]
        w1buf = [alloc(f"w1b{i}", [128, 1024], BF16, E_OFF + 20480 + i * 2048) for i in range(2)]
        w2buf = [alloc(f"w2b{i}", [128, 4096], BF16, C_OFF + i * 8192) for i in range(2)]
        phi_q = alloc("phi_q", [128, 8 * TOWN], BF16, F_OFF)
        h_lo = alloc("h_lo", [128, 16 * 1024], BF16, F_OFF)
        wk_bf = alloc("wk_bf", [128, 8 * 1024], BF16, F_OFF)
        wv_bf = alloc("wv_bf", [128, 8 * 1024], BF16, F_OFF + 16384)
        g_xt = [alloc(f"g_xt{i}", [128, 1024], F32, G_OFF + i * 4096) for i in range(2)]
        g_xnt = [alloc(f"g_xnt{i}", [128, 1024], BF16, G_OFF + 8192 + i * 2048) for i in range(2)]
        g_xnf = [alloc(f"g_xnf{i}", [128, 1024], BF16, G_OFF + 12288 + i * 2048) for i in range(2)]
        g_e = [alloc(f"g_e{i}", [128, 1024], BF16, G_OFF + 16384 + i * 2048) for i in range(2)]
        g_pk = [alloc(f"g_pk{i}", [128, 1024], BF16, G_OFF + 20480 + i * 2048) for i in range(2)]
        g_pv = [alloc(f"g_pv{i}", [128, 1024], BF16, G_OFF + 24576 + i * 2048) for i in range(2)]
        g_r = [alloc(f"g_r{i}", [128, 1024], BF16, G_OFF + 28672 + i * 2048) for i in range(2)]
        wq_bf = alloc("wq_bf", [128, 8 * 1024], BF16, G_OFF)
        y_f = alloc("y_f", [128, 8 * TOWN], BF16, G_OFF)
        h_hi = alloc("h_hi", [128, 16 * 1024], BF16, G_OFF)
        s_loc = alloc("s_loc", [128, NOWN * 520], BF16, C_OFF)

        moff = [M_OFF]
        def malloc(name, shape, dt):
            esz = {F32: 4, BF16: 2}[dt]
            per_part = int(np.prod(shape[1:])) * esz
            off = moff[0]
            moff[0] = off + ((per_part + 31) // 32) * 32
            assert moff[0] <= SBUF_LIMIT, f"misc overflow at {name}"
            return nc.alloc_sbuf_tensor_at(name, list(shape), dt, offset=off)

        qw_buf = [malloc(f"qw{i}", [128, 1024], F32) for i in range(3)]
        qs_buf = [malloc(f"qs{i}", [128, 1024], BF16) for i in range(2)]
        qo_buf = [malloc(f"qo{i}", [128, 1024], BF16) for i in range(2)]
        s_run = malloc("s_run", [128, 520], F32)
        kf_buf = [malloc(f"kf{i}", [128, 8 * 128], BF16) for i in range(2)]
        am_buf = [malloc(f"am{i}", [128, 128], BF16) for i in range(3)]
        stat6 = [malloc(f"st6{i}", [128, 12], F32) for i in range(2)]
        mv_b = [malloc(f"mv{i}", [128, 2], F32) for i in range(2)]
        lnv_b = [malloc(f"lnv{i}", [128, 1], F32) for i in range(2)]
        rstd_b = [malloc(f"rstd{i}", [128, 1], F32) for i in range(2)]
        zc_b = [malloc(f"zc{i}", [128, 1], F32) for i in range(3)]
        zr_b = [malloc(f"zr{i}", [128, 1], F32) for i in range(3)]
        acc_q = malloc("acc_q", [128, 2], F32)
        accp = malloc("accp", [128, 1], F32)
        t_row = malloc("t_row", [1, 2], F32)
        tb_w = [malloc(f"tbw{i}", [128, 1], F32) for i in range(2)]
        tn_w = [malloc(f"tnw{i}", [128, 1], F32) for i in range(2)]
        b1x2 = malloc("b1x2", [128, 32], F32)
        sb_maskT = malloc("sb_maskT", [128, 128], F32)
        sb_cmask = malloc("sb_cmask", [128, NT], BF16)
        sb_onesb = malloc("sb_onesb", [128, 1], BF16)
        sb_onesf = malloc("sb_onesf", [128, 1], F32)
        sb_onesfr = malloc("sb_onesfr", [1, 128], F32)
        sb_idn = malloc("sb_idn", [128, 128], BF16)
        sb_b1 = malloc("sb_b1", [128, 32], F32)
        sb_b2 = malloc("sb_b2", [128, 8], F32)

        pb = [ctx.enter_context(nc.psum_tensor(f"pb{i}", [128, 512], F32)) for i in range(8)]

        def mksem(name):
            return Sem(ctx.enter_context(nc.semaphore(name)))

        def mksem2(name):
            return [mksem(name + "a"), mksem(name + "b")]

        sCONST = mksem("sCONST")
        sXT = mksem2("sXT")
        sW = mksem("sW")
        sWB = mksem2("sWB")
        sQ4 = [mksem(f"sQ4_{i}") for i in range(3)]
        sQ1 = mksem2("sQ1")
        sQ2 = mksem2("sQ2")
        sQS = mksem2("sQS")
        sTR = mksem2("sTR")
        sTRS = mksem2("sTRS")
        sKF = mksem2("sKF")
        sYT = mksem("sYT")
        sST = mksem2("sST")
        sXO = mksem2("sXO")
        sW1 = mksem2("sW1")
        sW2 = mksem2("sW2")
        sTAP = mksem("sTAP")
        sTAPB = mksem("sTAPB")
        sPE = mksem("sPE")
        sVE = mksem("sVE")
        sSE = mksem("sSE")
        sPO = mksem("sPO")

        B = Builder()
        op = B.op

        def ld(dst, src, waits=()):
            return op("sp", lambda e, d=dst, s=src: e.dma_start(out=d, in_=s),
                      waits=waits, inc=sCONST, inc_by=16)

        ld(sb_maskT[:, :], d_in["maskT"][:, :])
        ld(sb_cmask[:, :], d_in["cmask"][:, :])
        ld(sb_onesb[:, :], d_in["onesb"][:, :])
        ld(sb_onesf[:, :], d_in["onesf"][:, :])
        ld(sb_onesfr[:, :], d_in["onesfr"][:, :])
        ld(sb_idn[:, :], d_in["idn"][:, :])
        ld(sb_b1[:, :], d_in["b1m"][:, :])
        tok_consts = ld(sb_b2[:, :], d_in["b2m"][:, :])

        def wload(dst, name, waits=()):
            return op("ge", lambda e, d=dst, n=name: e.dma_start(
                out=d.rearrange("p (a d) -> p a d", a=8),
                in_=d_in[n].rearrange("(a p) d -> p a d", p=128)),
                waits=waits, inc=sW, inc_by=16)

        tok_wk = wload(wk_bf, "wkT")
        tok_wv = wload(wv_bf, "wvT")

        tok_b1x2 = op("ve", lambda e: e.tensor_scalar(b1x2[:, :], sb_b1[:, :], 2.0, None, ALU.mult),
                      waits=[tok_consts], inc=sVE)
        tok_srun0 = op("ve", lambda e: e.memset(s_run[:, :], 0.0), inc=sVE)
        tok_acc0 = op("ve", lambda e: e.memset(acc_q[:, :], 0.0), inc=sVE)

        q_tiles = []
        for r in range(8):
            for c in range(4):
                q_tiles.append((0, (r, c)))
        for r in range(32):
            q_tiles.append((1, (r,)))

        def q_src(m, idx):
            if m == 0:
                r, c = idx
                return d_in["w1T"][r * 128:(r + 1) * 128, c * 1024:(c + 1) * 1024]
            (r,) = idx
            return d_in["w2T"][r * 128:(r + 1) * 128, :]

        def q_dst(m, idx):
            if m == 0:
                r, c = idx
                return wq1_d[r * 128:(r + 1) * 128, c * 1024:(c + 1) * 1024]
            (r,) = idx
            return wq2_d[r * 128:(r + 1) * 128, :]

        p1_state = {"red": [None, None], "acc": tok_acc0}

        def emit_p1(i):
            m, idx = q_tiles[i]
            tk = op("ge", lambda e, i=i, m=m, idx=idx: e.dma_start(
                out=qw_buf[i % 2][:, :], in_=q_src(m, idx)),
                waits=[p1_state["red"][i % 2]], inc=sQ1[i % 2], inc_by=16)
            t1 = op("ve", lambda e, i=i: e.tensor_reduce(
                accp[:, :], qw_buf[i % 2][:, :], AX.X, ALU.add, apply_absolute_value=True),
                waits=[tk, p1_state["acc"]], inc=sVE)
            p1_state["acc"] = op("po", lambda e, m=m: e.tensor_tensor(
                acc_q[:, m:m + 1], accp[:, :], acc_q[:, m:m + 1], ALU.add),
                waits=[t1], inc=sPO)
            p1_state["red"][i % 2] = p1_state["acc"]

        def stop_after(phase, toks):
            if BUILD_UPTO == phase:
                op("sp", lambda e: None, waits=toks)
                blk = ctx.enter_context(nc.Block())
                blk.sync(lambda e: B.run(e, "sp"))
                blk.gpsimd(lambda e: B.run(e, {"ge", "po"}))
                blk.vector(lambda e: B.run(e, "ve"))
                blk.scalar(lambda e: B.run(e, "se"))
                blk.tensor(lambda e: B.run(e, "te"))
                return True
            return False

        if stop_after("loads", [tok_consts, tok_wk, tok_wv, tok_b1x2]):
            return nc

        tok_xt = [None] * NT
        tok_apply = [None] * NT
        tok_trp = [None] * NT
        tok_xnf = [None] * NT
        tok_proj = [None] * NT
        tok_phi = [None] * NT
        tok_vev = [None] * NT
        tok_kv = [None] * NT
        tok_sadd = [None] * NT

        def tile_refs(n):
            par = n % 2
            pset = pb[4 * par: 4 * par + 4]
            own = n >= OWN0
            oc = n - OWN0
            pdst = phik_own[:, oc * 1024:(oc + 1) * 1024] if own else g_pk[n % 2][:, :]
            vdst = v_own[:, oc * 1024:(oc + 1) * 1024] if own else g_pv[n % 2][:, :]
            return pset, own, oc, pdst, vdst

        def xnf_dst(n, dc):
            if n >= OWN0:
                oc = n - OWN0
                return xn_own[:, dc * TOWN + oc * CH: dc * TOWN + oc * CH + CH]
            return g_xnf[n % 2][:, dc * CH:(dc + 1) * CH]

        def emit_xt(n):
            tok_xt[n] = op("sp", lambda e, n=n: e.dma_start(
                out=g_xt[n % 2][:, :], in_=d_in["x_seq"][n * CH:(n + 1) * CH, :]),
                waits=[tok_apply[n - 2] if n >= 2 else None], inc=sXT[n % 2], inc_by=16)

        def emit_ln(n):
            tv1 = op("ve", lambda e, n=n: e.bn_stats(stat6[n % 2][:, 0:6], g_xt[n % 2][:, 0:512]),
                     waits=[tok_xt[n]], inc=sVE)
            tv2 = op("ve", lambda e, n=n: e.bn_stats(stat6[n % 2][:, 6:12], g_xt[n % 2][:, 512:1024]),
                     waits=[tv1], inc=sVE)
            tv3 = op("ve", lambda e, n=n: e.bn_aggr(mv_b[n % 2][:, :], stat6[n % 2][:, :]),
                     waits=[tv2], inc=sVE)
            tv4 = op("ve", lambda e, n=n: e.tensor_scalar(
                lnv_b[n % 2][:, :], mv_b[n % 2][:, 1:2], LNEPS, None, ALU.add),
                waits=[tv3], inc=sVE)
            ts1 = op("se", lambda e, n=n: e.activation(
                rstd_b[n % 2][:, :], lnv_b[n % 2][:, :], AF.Ln), waits=[tv4], inc=sSE)
            ts2 = op("se", lambda e, n=n: e.activation(
                rstd_b[n % 2][:, :], rstd_b[n % 2][:, :], AF.Exp, scale=-0.5),
                waits=[ts1], inc=sSE)
            tok_apply[n] = op("ve", lambda e, n=n: e.tensor_scalar(
                g_xnt[n % 2][:, :], g_xt[n % 2][:, :], mv_b[n % 2][:, 0:1], rstd_b[n % 2][:, 0:1],
                ALU.subtract, ALU.mult),
                waits=[ts2, tok_trp[n - 2] if n >= 2 else None], inc=sVE)

        def emit_tr(n):
            pset, own, oc, _, _ = tile_refs(n)
            w0 = [tok_apply[n], tok_consts,
                  tok_vev[n - 2] if n >= 2 else None]

            def trp(e, n=n, pset=pset):
                for dc in range(8):
                    bank = pset[2 + dc // 4]
                    ins = e.matmul(bank[:, (dc % 4) * 128:(dc % 4 + 1) * 128],
                                   g_xnt[n % 2][:, dc * CH:(dc + 1) * CH], sb_idn[:, :],
                                   start=True, stop=True, skip_group_check=True)
                return ins
            tok_trp[n] = op("te", trp, waits=w0, inc=sPE)
            last = None
            for dc in range(8):
                bank = pset[2 + dc // 4]
                last = op("se", lambda e, n=n, dc=dc, bank=bank: e.activation(
                    xnf_dst(n, dc), bank[:, (dc % 4) * 128:(dc % 4 + 1) * 128], AF.Copy),
                    waits=[tok_trp[n], tok_proj[n - 2] if n >= 2 else None], inc=sSE)
            tok_xnf[n] = last

        def emit_proj(n):
            pset, own, oc, _, _ = tile_refs(n)
            w0 = [tok_xnf[n], tok_wk, tok_wv]
            if n >= 2:
                w0 += [tok_phi[n - 2], tok_vev[n - 2], tok_sadd[n - 2]]

            def proj(e, n=n, own=own, oc=oc, pset=pset):
                for dc in range(8):
                    if own:
                        lhs = xn_own[:, dc * TOWN + oc * CH: dc * TOWN + oc * CH + CH]
                    else:
                        lhs = g_xnf[n % 2][:, dc * CH:(dc + 1) * CH]
                    st, sp = (dc == 0), (dc == 7)
                    e.matmul(pset[0][:, :], lhs, wk_bf[:, dc * 1024:dc * 1024 + 512],
                             start=st, stop=sp, skip_group_check=True)
                    e.matmul(pset[1][:, :], lhs, wk_bf[:, dc * 1024 + 512:(dc + 1) * 1024],
                             start=st, stop=sp, skip_group_check=True)
                    e.matmul(pset[2][:, :], lhs, wv_bf[:, dc * 1024:dc * 1024 + 512],
                             start=st, stop=sp, skip_group_check=True)
                    ins = e.matmul(pset[3][:, :], lhs, wv_bf[:, dc * 1024 + 512:(dc + 1) * 1024],
                                   start=st, stop=sp, skip_group_check=True)
                return ins
            tok_proj[n] = op("te", proj, waits=w0, inc=sPE)

        def emit_phiv(n):
            pset, own, oc, pdst, vdst = tile_refs(n)
            tvv = op("se", lambda e, pset=pset, d=vdst: e.activation(
                d[:, 0:512], pset[2][:, :], AF.Copy),
                waits=[tok_proj[n], tok_phi[n - 2] if n >= 2 else None], inc=sSE)
            tok_vev[n] = op("se", lambda e, pset=pset, d=vdst: e.activation(
                d[:, 512:1024], pset[3][:, :], AF.Copy), waits=[tvv], inc=sSE)
            te1 = op("se", lambda e, pset=pset, n=n: e.activation(
                g_e[n % 2][:, 0:512], pset[0][:, :], AF.Exp), waits=[tok_vev[n]], inc=sSE)
            te2 = op("se", lambda e, pset=pset, n=n: e.activation(
                g_e[n % 2][:, 512:1024], pset[1][:, :], AF.Exp), waits=[te1], inc=sSE)
            tm1 = op("ve", lambda e, pset=pset, n=n, d=pdst: e.tensor_scalar(
                d[:, 0:512], pset[0][:, :], 0.0, None, ALU.max),
                waits=[tok_proj[n], te1], inc=sVE)
            tx1 = op("ve", lambda e, n=n, d=pdst: e.scalar_tensor_tensor(
                d[:, 0:512], g_e[n % 2][:, 0:512], 1.0, d[:, 0:512], ALU.min, ALU.add),
                waits=[tm1], inc=sVE)
            tm2 = op("ve", lambda e, pset=pset, n=n, d=pdst: e.tensor_scalar(
                d[:, 512:1024], pset[1][:, :], 0.0, None, ALU.max),
                waits=[te2], inc=sVE)
            tok_phi[n] = op("ve", lambda e, n=n, d=pdst: e.scalar_tensor_tensor(
                d[:, 512:1024], g_e[n % 2][:, 512:1024], 1.0, d[:, 512:1024],
                ALU.min, ALU.add), waits=[tm2], inc=sVE)

        def emit_kv(m):
            pset, own, oc, pdst, vdst = tile_refs(m)

            def kvmm(e, m=m, pset=pset, pdst=pdst, vdst=vdst):
                for h in range(HEADS):
                    bank = pset[0] if h < 8 else pset[1]
                    rows = slice((h % 2) * 64, (h % 2) * 64 + 64)
                    blk_ = (h // 2) % 4
                    lhs = pdst[:, h * 64:(h + 1) * 64]
                    st = (h == 0 or h == 8)
                    e.matmul(bank[rows, blk_ * 65: blk_ * 65 + 64], lhs,
                             vdst[:, h * 64:(h + 1) * 64], start=st, stop=False,
                             skip_group_check=True)
                    ins = e.matmul(bank[rows, blk_ * 65 + 64: blk_ * 65 + 65], lhs,
                                   sb_cmask[:, m:m + 1], start=False, stop=(h == 15),
                                   skip_group_check=True)
                return ins
            tok_kv[m] = op("te", kvmm, waits=[tok_phi[m], tok_vev[m], tok_consts], inc=sPE)

            wv_ = [tok_kv[m], tok_srun0, tok_sadd[m - 1] if m > 0 else None]
            if own:
                tsl = op("ve", lambda e, oc=oc: e.tensor_copy(
                    s_loc[:, oc * 520:(oc + 1) * 520], s_run[:, :]),
                    waits=[tok_sadd[m - 1] if m > 0 else tok_srun0], inc=sVE)
                wv_.append(tsl)
            ta1 = op("ve", lambda e, pset=pset: e.scalar_tensor_tensor(
                s_run[:, 0:260], pset[0][:, 0:260], 1.0, s_run[:, 0:260], ALU.mult, ALU.add),
                waits=wv_, inc=sVE)
            tok_sadd[m] = op("ve", lambda e, pset=pset: e.scalar_tensor_tensor(
                s_run[:, 260:520], pset[1][:, 0:260], 1.0, s_run[:, 260:520], ALU.mult, ALU.add),
                waits=[ta1], inc=sVE)

        emit_xt(0)
        emit_ln(0)
        for slot in range(NT + 2):
            if slot + 1 < NT:
                emit_xt(slot + 1)
            if slot < NT:
                emit_tr(slot)
            if slot + 1 < NT:
                emit_ln(slot + 1)
            if 1 <= slot <= NT:
                emit_proj(slot - 1)
            if slot >= 2:
                emit_kv(slot - 2)
            if 1 <= slot <= NT:
                emit_phiv(slot - 1)
            if slot < NT:
                emit_p1(slot)

        tok_scan_pe = tok_kv[NT - 1]
        tok_scan_done = tok_sadd[NT - 1]
        tok_p1_done = p1_state["acc"]

        if stop_after("scan", [tok_scan_pe, tok_scan_done, tok_phi[NT - 1], tok_vev[NT - 1],
                               tok_xnf[NT - 1], tok_p1_done]):
            return nc

        tok_tot = op("te", lambda e: e.matmul(pb[0][0:1, 0:2], sb_onesf[:, :], acc_q[:, :],
                                              start=True, stop=True, skip_group_check=True),
                     waits=[tok_p1_done, tok_scan_pe, tok_phi[NT - 1], tok_vev[NT - 1],
                            tok_sadd[NT - 1]], inc=sPE)
        kq = 0.5 / (DFF * D)
        tok_trow = op("ve", lambda e: e.tensor_scalar(
            t_row[0:1, :], pb[0][0:1, 0:2], kq, 0.5 * QEPS, ALU.mult, ALU.add),
            waits=[tok_tot], inc=sVE)
        if "tvals" in taps:
            op("sp", lambda e: e.dma_start(out=taps["tvals"][:, :], in_=t_row[0:1, :]),
               waits=[tok_trow], inc=sTAPB, inc_by=16)
        tok_bc = [None, None]
        for m in range(2):
            tt = op("te", lambda e, m=m: e.matmul(
                pb[1][:, m:m + 1], sb_onesfr[0:1, :], t_row[0:1, m:m + 1],
                start=(m == 0), stop=(m == 1), skip_group_check=True),
                waits=[tok_trow], inc=sPE)
            t1 = op("ve", lambda e, m=m: e.tensor_copy(tb_w[m][:, :], pb[1][:, m:m + 1]),
                    waits=[tt], inc=sVE)
            tok_bc[m] = op("ve", lambda e, m=m: e.tensor_scalar(
                tn_w[m][:, :], tb_w[m][:, :], -1.0, None, ALU.mult), waits=[t1], inc=sVE)

        if stop_after("boundary", [tok_bc[0], tok_bc[1]]):
            return nc

        p2_state = {"sign": [None, None], "signq": [None] * 3, "store": [None, None],
                    "pend": [None, None]}

        def flush_p2_store(par):
            if p2_state["pend"][par] is None:
                return
            si, tadd = p2_state["pend"][par]
            p2_state["pend"][par] = None
            m, idx = q_tiles[si]
            p2_state["store"][par] = op("ge", lambda e, si=si, m=m, idx=idx: e.dma_start(
                out=q_dst(m, idx), in_=qo_buf[si % 2][:, :]), waits=[tadd],
                inc=sQS[par], inc_by=16)
            if si == 31:
                p2_state["w1a"] = p2_state["store"][0]
                p2_state["w1b"] = p2_state["store"][1]

        def emit_p2(i, addeng="ve"):
            m, idx = q_tiles[i]
            tk = op("ge", lambda e, i=i, m=m, idx=idx: e.dma_start(
                out=qw_buf[i % 3][:, :], in_=q_src(m, idx)),
                waits=[tok_bc[1], p2_state["signq"][i % 3]], inc=sQ4[i % 3], inc_by=16)
            flush_p2_store(i % 2)
            tsa = op("se", lambda e, i=i, m=m: e.activation(
                qs_buf[i % 2][:, :], qw_buf[i % 3][:, :], AF.Sign, bias=tn_w[m][:, 0:1]),
                waits=[tk, p2_state["store"][i % 2],
                       p2_state.get("tap0") if i == 2 else None], inc=sSE)
            tsb = op("se", lambda e, i=i, m=m: e.activation(
                qo_buf[i % 2][:, :], qw_buf[i % 3][:, :], AF.Sign, bias=tb_w[m][:, 0:1]),
                waits=[tsa], inc=sSE)
            p2_state["sign"][i % 2] = tsb
            p2_state["signq"][i % 3] = tsb
            if addeng == "po":
                tadd = op("po", lambda e, i=i: e.tensor_tensor(
                    qo_buf[i % 2][:, :], qs_buf[i % 2][:, :], qo_buf[i % 2][:, :],
                    ALU.add), waits=[tsb], inc=sPO)
            else:
                tadd = op("ve", lambda e, i=i: e.scalar_tensor_tensor(
                    qo_buf[i % 2][:, :], qs_buf[i % 2][:, :], 1.0, qo_buf[i % 2][:, :],
                    ALU.mult, ALU.add), waits=[tsb], inc=sVE)
            if "wq1" in taps and i == 0:
                p2_state["tap0"] = op(
                    "sp", lambda e: e.dma_start(out=taps["wq1"][:, :], in_=qo_buf[0][:, :]),
                    waits=[tadd], inc=sTAPB, inc_by=16)
            p2_state["pend"][i % 2] = (i, tadd)

        tok_wqb = [None] * 8
        for dcb in range(8):
            wts = [tok_scan_pe, tok_scan_done, tok_phi[NT - 1], tok_vev[NT - 1],
                   tok_xnf[NT - 1], tok_apply[NT - 1]] if dcb == 0 else []
            if dcb >= 2:
                wts.append(tok_wqb[dcb - 2])
            tok_wqb[dcb] = op("ge", lambda e, dcb=dcb: e.dma_start(
                out=wq_bf[:, dcb * 1024:(dcb + 1) * 1024].rearrange("p (a d) -> p a d", a=8),
                in_=d_in["wqT"][:, dcb * 128:(dcb + 1) * 128].rearrange("(a p) d -> p a d", p=128)),
                waits=wts, inc=sWB[dcb % 2], inc_by=16)

        tok_qphi = [None] * 8
        tok_qproj = [None] * 8
        qphi_par = [None, None]
        for dc in range(8):
            par = dc % 2
            pset = pb[4 * par: 4 * par + 4]
            w0 = [tok_wqb[dc], tok_xnf[NT - 1], tok_bc[1], tok_tot]
            if dc >= 2:
                w0 += [qphi_par[0], qphi_par[1]]

            def qproj(e, dc=dc, pset=pset):
                for di in range(8):
                    lhs = wq_bf[:, dc * 1024 + di * 128: dc * 1024 + (di + 1) * 128]
                    st, sp = (di == 0), (di == 7)
                    for ts in range(4):
                        ins = e.matmul(pset[ts][:, :], lhs,
                                       xn_own[:, di * TOWN + ts * 512: di * TOWN + (ts + 1) * 512],
                                       start=st, stop=sp, skip_group_check=True)
                return ins
            tok_qproj[dc] = op("te", qproj, waits=w0, inc=sPE)
            for ts in range(4):
                dst = phi_q[:, dc * TOWN + ts * 512: dc * TOWN + (ts + 1) * 512]
                bi = ts % 2
                te1 = op("se", lambda e, pset=pset, ts=ts, bi=bi: e.activation(
                    g_e[bi][:, 0:512], pset[ts][:, :], AF.Exp),
                    waits=[tok_qproj[dc], qphi_par[bi]], inc=sSE)
                tm1 = op("ve", lambda e, pset=pset, ts=ts, d=dst: e.tensor_scalar(
                    d, pset[ts][:, :], 0.0, None, ALU.max),
                    waits=[tok_qproj[dc], te1], inc=sVE)
                qphi_par[bi] = op("ve", lambda e, bi=bi, d=dst: e.scalar_tensor_tensor(
                    d, g_e[bi][:, 0:512], 1.0, d, ALU.min, ALU.add),
                    waits=[tm1], inc=sVE)
            tok_qphi[dc] = qphi_par[(len(range(4)) - 1) % 2]
            emit_p2(dc)
        tok_qdone = tok_qphi[7]

        tap_gate = None
        for tname, buf, tokw in [("xn", xn_own, tok_xnf[NT - 1]),
                                 ("phik", phik_own, tok_phi[NT - 1]),
                                 ("v", v_own, tok_vev[NT - 1]),
                                 ("sloc", s_loc, tok_scan_done),
                                 ("phiq", phi_q, tok_qdone)]:
            if tname in taps:
                tap_gate = op("sp", lambda e, b=buf, t=tname: e.dma_start(
                    out=taps[t][:, :], in_=b[:, :]), waits=[tokw], inc=sTAP, inc_by=16)

        if stop_after("qproj", [tok_qdone]):
            return nc

        tok_attnT = {}
        tok_am = {}
        tok_intra = {}
        tok_y = {}
        tok_kfT = [None] * NOWN
        tok_ytr = [None] * NOWN

        units = [(c, h) for c in range(NOWN) for h in range(HEADS)]

        def unit_refs(c, h):
            u = c * HEADS + h
            pa = [pb[0], pb[1], pb[4]][u % 3]
            py = [pb[2], pb[3], pb[5]][u % 3]
            rows = slice((h % 2) * 64, (h % 2) * 64 + 64)
            qsl = phi_q[rows, (h // 2) * TOWN + c * CH: (h // 2) * TOWN + (c + 1) * CH]
            ksl = kf_buf[c % 2][rows, (h // 2) * 128:(h // 2) * 128 + 128]
            vsl = v_own[:, c * 1024 + h * 64: c * 1024 + (h + 1) * 64]
            ssl = s_loc[rows, c * 520 + (h // 2) * 65: c * 520 + (h // 2) * 65 + 65]
            return u, pa, py, qsl, ksl, vsl, ssl

        def prev_tok(dct, c, h, k=2):
            u = c * HEADS + h - k
            if u < 0:
                return None
            return dct.get((u // HEADS, u % HEADS))

        def emit_head(c, h):
            u, pa, py, qsl, ksl, vsl, ssl = unit_refs(c, h)
            tok_attnT[(c, h)] = op("te", lambda e, pa=pa, ksl=ksl, qsl=qsl: e.matmul(
                pa[:, 0:128], ksl, qsl, start=True, stop=True, skip_group_check=True),
                waits=[tok_kfT[c], tok_qdone, prev_tok(tok_am, c, h, 3)], inc=sPE)
            tok_am[(c, h)] = op("ve", lambda e, u=u, pa=pa: e.tensor_tensor(
                am_buf[u % 3][:, :], pa[:, 0:128], sb_maskT[:, :], ALU.mult),
                waits=[tok_attnT[(c, h)], prev_tok(tok_intra, c, h, 3)], inc=sVE)

        def emit_tail(c, h):
            u, pa, py, qsl, ksl, vsl, ssl = unit_refs(c, h)

            def intra(e, u=u, py=py, vsl=vsl, qsl=qsl, ssl=ssl):
                e.matmul(py[:, 0:64], am_buf[u % 3][:, :], vsl, start=True, stop=False,
                         skip_group_check=True)
                e.matmul(py[:, 64:65], am_buf[u % 3][:, :], sb_onesb[:, :], start=False,
                         stop=False, skip_group_check=True)
                return e.matmul(py[:, 0:65], qsl, ssl, start=False, stop=True,
                                skip_group_check=True)
            tok_intra[(c, h)] = op("te", intra,
                                   waits=[tok_am[(c, h)], tok_scan_done,
                                          prev_tok(tok_y, c, h, 3)], inc=sPE)
            tz1 = op("ve", lambda e, u=u, py=py: e.tensor_scalar(
                zc_b[u % 3][:, :], py[:, 64:65], 1e-6, None, ALU.max),
                waits=[tok_intra[(c, h)], prev_tok(tok_y, c, h, 3)], inc=sVE)
            tz2 = op("ve", lambda e, u=u: e.reciprocal(zr_b[u % 3][:, :], zc_b[u % 3][:, :]),
                     waits=[tz1], inc=sVE)
            tok_y[(c, h)] = op("se", lambda e, u=u, py=py, c=c, h=h: e.activation(
                y_t[:, c * 1024 + h * 64: c * 1024 + (h + 1) * 64], py[:, 0:64],
                AF.Copy, scale=zr_b[u % 3][:, 0:1]),
                waits=[tz2, tok_kfT[c], tap_gate], inc=sSE)
            if h == 15:
                pair = [pb[6], pb[7]]
                w0 = [tok_y[(c, 15)], tok_qdone, trp_state["evac"]]

                def ytr(e, c=c, pair=pair):
                    for ec in range(8):
                        bank = pair[ec // 4]
                        ins = e.matmul(bank[:, (ec % 4) * 128:(ec % 4 + 1) * 128],
                                       y_t[:, c * 1024 + ec * 128: c * 1024 + (ec + 1) * 128],
                                       sb_idn[:, :], start=True, stop=True,
                                       skip_group_check=True)
                    return ins
                tpe_y = op("te", ytr, waits=w0, inc=sPE)
                last = None
                for ec in range(8):
                    bank = pair[ec // 4]
                    last = op("se", lambda e, c=c, ec=ec, bank=bank: e.activation(
                        y_f[:, ec * TOWN + c * CH: ec * TOWN + (c + 1) * CH],
                        bank[:, (ec % 4) * 128:(ec % 4 + 1) * 128], AF.Copy),
                        waits=[tpe_y, last], inc=sSE)
                tok_ytr[c] = last
                trp_state["evac"] = last
                emit_p2(8 + (3 * c + 1) // 2, "po")
                if c % 2 == 0:
                    emit_p2(9 + (3 * c + 1) // 2, "po")

        trp_state = {"evac": tok_qdone}
        for i, (c, h) in enumerate(units):
            if h == 0:
                pair = [pb[6], pb[7]]
                w0 = [tok_phi[OWN0 + c], tok_qdone,
                      tok_attnT.get((c - 2, 15)) if c >= 2 else None,
                      trp_state["evac"]]

                def kft(e, c=c, pair=pair):
                    for fc in range(8):
                        bank = pair[fc // 4]
                        ins = e.matmul(bank[:, (fc % 4) * 128:(fc % 4 + 1) * 128],
                                       phik_own[:, c * 1024 + fc * 128: c * 1024 + (fc + 1) * 128],
                                       sb_idn[:, :], start=True, stop=True,
                                       skip_group_check=True)
                    return ins
                tpe_k = op("te", kft, waits=w0, inc=sPE)
                last = None
                for fc in range(8):
                    bank = pair[fc // 4]
                    if fc < 4:
                        last = op("ve", lambda e, c=c, fc=fc, bank=bank: e.tensor_copy(
                            kf_buf[c % 2][:, fc * 128:(fc + 1) * 128],
                            bank[:, (fc % 4) * 128:(fc % 4 + 1) * 128]),
                            waits=[tpe_k, last], inc=sVE)
                    else:
                        last = op("se", lambda e, c=c, fc=fc, bank=bank: e.activation(
                            kf_buf[c % 2][:, fc * 128:(fc + 1) * 128],
                            bank[:, (fc % 4) * 128:(fc % 4 + 1) * 128], AF.Copy),
                            waits=[tpe_k, last], inc=sSE)
                tok_kfT[c] = last
                trp_state["evac"] = last
            emit_head(c, h)
            if i > 1:
                emit_tail(*units[i - 2])
        emit_tail(*units[-2])
        emit_tail(*units[-1])

        tap_gate2 = None
        if "yt" in taps:
            tap_gate2 = op("sp", lambda e: e.dma_start(out=taps["yt"][:, :], in_=y_t[:, :]),
                           waits=[tok_y[(NOWN - 1, 15)]], inc=sTAP, inc_by=16)
        tap_gate3 = None
        if "yf" in taps:
            tap_gate3 = op("sp", lambda e: e.dma_start(out=taps["yf"][:, :], in_=y_f[:, :]),
                           waits=[tok_ytr[NOWN - 1]], inc=sTAP, inc_by=16)

        if stop_after("phb", [tok_y[(NOWN - 1, 15)], tok_ytr[NOWN - 1]]):
            return nc

        tok_wob = [None] * 8
        for dcb in range(8):
            wts = [tok_intra[(NOWN - 1, 15)], tok_y[(NOWN - 1, 15)], tap_gate,
                   tok_wqb[6 + dcb % 2] if dcb < 2 else tok_wob[dcb - 2]]
            tok_wob[dcb] = op("ge", lambda e, dcb=dcb: e.dma_start(
                out=wo_sb[:, dcb * 1024:(dcb + 1) * 1024].rearrange("p (a d) -> p a d", a=8),
                in_=d_in["woT"][:, dcb * 128:(dcb + 1) * 128].rearrange("(a p) d -> p a d", p=128)),
                waits=wts, inc=sWB[dcb % 2], inc_by=16)

        tok_wos = [None] * 8
        for dc in range(8):
            par = dc % 2
            pset = pb[4 * par: 4 * par + 4]
            w0 = [tok_wob[dc], tok_ytr[NOWN - 1],
                  tok_y[(NOWN - 1, 15)], tok_am[(NOWN - 1, 15)]]
            if dc >= 2:
                w0.append(tok_wos[dc - 2])

            def wos(e, dc=dc, pset=pset):
                for di in range(8):
                    lhs = wo_sb[:, dc * 1024 + di * 128: dc * 1024 + (di + 1) * 128]
                    st, sp = (di == 0), (di == 7)
                    for ts in range(4):
                        ins = e.matmul(pset[ts][:, :], lhs,
                                       y_f[:, di * TOWN + ts * 512: di * TOWN + (ts + 1) * 512],
                                       start=st, stop=sp, skip_group_check=True)
                return ins
            tpe = op("te", wos, waits=w0, inc=sPE)
            lastv = None
            for ts in range(4):
                lastv = op("ve", lambda e, dc=dc, ts=ts, pset=pset: e.tensor_copy(
                    s_sb[:, dc * TOWN + ts * 512: dc * TOWN + (ts + 1) * 512], pset[ts][:, :]),
                    waits=[tpe, lastv, tap_gate2], inc=sVE)
            tok_wos[dc] = lastv
            emit_p2(32 + dc)
        tok_s = tok_wos[7]
        tok_p2a = p2_state["store"][0]
        tok_p2b = p2_state["store"][1]

        tap_gate4 = None
        if "s" in taps:
            tap_gate4 = op("sp", lambda e: e.dma_start(out=taps["s"][:, :], in_=s_sb[:, :]),
                           waits=[tok_s], inc=sTAP, inc_by=16)
        tap_gate5 = None

        if stop_after("wo", [tok_s]):
            return nc

        tok_hev = {}
        tok_gev = {}
        tok_comb = {}
        tok_out = {}
        tok_xo = {}
        tok_w1use = [None, None]
        tok_w2use = [None, None]
        tok_w1l = [None, None]
        tok_w2l = [None, None]

        def gprev(dct, tb, j, k=2, per=8):
            g = tb * per + j - k
            if g < 0:
                return None
            return dct.get((g // per, g % per))

        for tb in range(2):
            for fc in range(32):
                i = fc % 2
                gidx = tb * 32 + fc
                tok_w1l[i] = op("sp", lambda e, fc=fc, i=i: e.dma_start(
                    out=w1buf[i].rearrange("p (a f) -> p a f", a=8),
                    in_=wq1_d[:, fc * 128:(fc + 1) * 128].rearrange("(a p) f -> p a f", p=128)),
                    waits=[p2_state["w1a"], p2_state["w1b"], tok_s, tok_w1use[i]],
                    inc=sW1[i], inc_by=16)
                m3 = fc % 3
                hps = [pb[0 + 2 * m3], pb[1 + 2 * m3]]
                w0 = [tok_w1l[i], tok_s, tap_gate3, tap_gate4]
                pg = gidx - 3
                if pg >= 0:
                    w0.append(tok_hev.get((pg // 32, pg % 32)))
                if tb == 1 and fc < 3:
                    w0 += [tok_gev.get((0, 7)), tok_comb.get((0, 7)), tap_gate5]

                def hmm(e, tb=tb, i=i, hps=hps):
                    for di in range(8):
                        lhs = w1buf[i][:, di * 128:(di + 1) * 128]
                        st, sp = (di == 0), (di == 7)
                        e.matmul(hps[0][:, :], lhs,
                                 xn_own[:, di * TOWN + tb * 1024: di * TOWN + tb * 1024 + 512],
                                 start=st, stop=sp, skip_group_check=True)
                        ins = e.matmul(hps[1][:, :], lhs,
                                       xn_own[:, di * TOWN + tb * 1024 + 512: di * TOWN + (tb + 1) * 1024],
                                       start=st, stop=sp, skip_group_check=True)
                    return ins
                tpe = op("te", hmm, waits=w0, inc=sPE)
                tok_w1use[i] = tpe
                hdst = h_lo if fc < 16 else h_hi
                hoff = (fc % 16) * 1024
                t1 = op("se", lambda e, hps=hps, d=hdst, o=hoff, fc=fc: e.activation(
                    d[:, o:o + 512], hps[0][:, :], AF.Relu, bias=b1x2[:, fc:fc + 1]),
                    waits=[tpe, tok_b1x2], inc=sSE)
                tok_hev[(tb, fc)] = op("ve", lambda e, hps=hps, d=hdst, o=hoff, fc=fc: e.tensor_scalar(
                    d[:, o + 512:o + 1024], hps[1][:, :], b1x2[:, fc:fc + 1], 0.0,
                    ALU.add, ALU.max), waits=[t1], inc=sVE)
                if tb == 0 and fc % 2 == 0 and fc < 24:
                    emit_p2(40 + fc)
                    emit_p2(41 + fc)

            if "h0" in taps and tb == 0:
                tap_gate5 = op("sp", lambda e: e.dma_start(
                    out=taps["h0"][:, :], in_=h_lo[:, 0:1024]),
                    waits=[tok_hev[(0, 0)]], inc=sTAP, inc_by=16)

            if tb == 0:
                flush_p2_store(0)
                flush_p2_store(1)
            for dc in range(8):
                i = dc % 2
                m3 = dc % 3
                gps = [pb[0 + 2 * m3], pb[1 + 2 * m3]]
                tok_w2l[i] = op("ge", lambda e, dc=dc, i=i: e.dma_start(
                    out=w2buf[i].rearrange("p (a f) -> p a f", a=32),
                    in_=wq2_d[:, dc * 128:(dc + 1) * 128]
                    .rearrange("(a p) f -> p a f", p=128)),
                    waits=[p2_state["store"][0], p2_state["store"][1], tok_s,
                           tok_w2use[i]], inc=sW2[i], inc_by=16)
                w0 = [tok_w2l[i], tok_hev[(tb, 31)], gprev(tok_gev, tb, dc, 3)]

                def gmm(e, i=i, gps=gps):
                    for fc in range(32):
                        lhs = w2buf[i][:, fc * 128:(fc + 1) * 128]
                        hsrc = h_lo if fc < 16 else h_hi
                        hoff = (fc % 16) * 1024
                        st, sp = (fc == 0), (fc == 31)
                        e.matmul(gps[0][:, :], lhs, hsrc[:, hoff:hoff + 512],
                                 start=st, stop=sp, skip_group_check=True)
                        ins = e.matmul(gps[1][:, :], lhs, hsrc[:, hoff + 512:hoff + 1024],
                                       start=st, stop=sp, skip_group_check=True)
                    return ins
                tpe = op("te", gmm, waits=w0, inc=sPE)
                tok_w2use[i] = tpe

                tg1 = op("se", lambda e, gps=gps, i=i, dc=dc: e.activation(
                    g_buf[i][:, 0:512], gps[0][:, :], AF.Silu, bias=sb_b2[:, dc:dc + 1], scale=0.25),
                    waits=[tpe, gprev(tok_comb, tb, dc)], inc=sSE)
                tok_gev[(tb, dc)] = op("se", lambda e, gps=gps, i=i, dc=dc: e.activation(
                    g_buf[i][:, 512:1024], gps[1][:, :], AF.Silu, bias=sb_b2[:, dc:dc + 1],
                    scale=0.25), waits=[tg1], inc=sSE)

                tok_xo[(tb, dc)] = op("sp", lambda e, tb=tb, dc=dc, i=i: e.dma_start(
                    out=xo_buf[i][:, :],
                    in_=d_in["x_ownT"][dc * 128:(dc + 1) * 128, tb * 1024:(tb + 1) * 1024]),
                    waits=[tok_s, gprev(tok_comb, tb, dc)], inc=sXO[i], inc_by=16)

                tm = op("ve", lambda e, tb=tb, dc=dc, i=i: e.tensor_tensor(
                    t1_buf[i][:, :], g_buf[i][:, :],
                    s_sb[:, dc * TOWN + tb * 1024: dc * TOWN + (tb + 1) * 1024], ALU.mult),
                    waits=[tok_gev[(tb, dc)], tok_s, gprev(tok_comb, tb, dc),
                           gprev(tok_out, tb, dc)], inc=sVE)
                tok_comb[(tb, dc)] = op("ve", lambda e, i=i: e.scalar_tensor_tensor(
                    t1_buf[i][:, :], xo_buf[i][:, :], 1.0, t1_buf[i][:, :], ALU.mult, ALU.add),
                    waits=[tm, tok_xo[(tb, dc)]], inc=sVE)
                tok_out[(tb, dc)] = op("sp", lambda e, tb=tb, dc=dc, i=i: e.dma_start(
                    out=out_f[dc * 128:(dc + 1) * 128, tb * 1024:(tb + 1) * 1024],
                    in_=t1_buf[i][:, :]),
                    waits=[tok_comb[(tb, dc)]], inc=sST[i], inc_by=16)

        op("sp", lambda e: None, waits=[sST[0].tok(), sST[1].tok()])

        blk = ctx.enter_context(nc.Block())
        blk.sync(lambda e: B.run(e, "sp"))
        blk.gpsimd(lambda e: B.run(e, {"ge", "po"}))
        blk.vector(lambda e: B.run(e, "ve"))
        blk.scalar(lambda e: B.run(e, "se"))
        blk.tensor(lambda e: B.run(e, "te"))

    return nc


_cached = {}


def kernel(L0_attn_x, L0_attn_W_q, L0_attn_W_k, L0_attn_W_v, L0_attn_W_o,
           L0_ffn_W1, L0_ffn_b1, L0_ffn_W2, L0_ffn_b2):
    x = np.asarray(L0_attn_x, dtype=np.float32)
    if "nc" not in _cached:
        _cached["nc"] = build_nc()
    nc = _cached["nc"]

    wqT = np.ascontiguousarray(np.asarray(L0_attn_W_q, np.float32).T)
    wkT = np.ascontiguousarray(np.asarray(L0_attn_W_k, np.float32).T)
    wvT = np.ascontiguousarray(np.asarray(L0_attn_W_v, np.float32).T)
    woT = np.ascontiguousarray(np.asarray(L0_attn_W_o, np.float32).T)
    w1T = np.ascontiguousarray(np.asarray(L0_ffn_W1, np.float32).T)
    w2T = np.ascontiguousarray(np.asarray(L0_ffn_W2, np.float32).T)
    b1m = np.ascontiguousarray(np.asarray(L0_ffn_b1, np.float32).reshape(32, 128).T)
    b2m = np.ascontiguousarray(np.asarray(L0_ffn_b2, np.float32).reshape(8, 128).T)
    maskT = np.ascontiguousarray(np.triu(np.ones((128, 128), np.float32)))
    onesb = np.ones((128, 1), bfloat16)
    onesf = np.ones((128, 1), np.float32)
    onesfr = np.ones((1, 128), np.float32)
    idn = np.eye(128, dtype=bfloat16)

    in_maps = []
    for c in range(8):
        b, j = c // 4, c % 4
        npad = TFULL - TOWN - j * TOWN
        xs = np.zeros((TFULL, D), np.float32)
        if j > 0:
            xs[npad:TFULL - TOWN] = x[b, :j * TOWN]
        xs[TFULL - TOWN:] = x[b, j * TOWN:(j + 1) * TOWN]
        x_ownT = np.ascontiguousarray(x[b, j * TOWN:(j + 1) * TOWN].T)
        cm = np.zeros((128, NT), bfloat16)
        cm[:, npad // CH:] = 1.0
        in_maps.append({
            "x_seq": xs, "x_ownT": x_ownT,
            "wqT": wqT, "wkT": wkT, "wvT": wvT, "woT": woT,
            "w1T": w1T, "w2T": w2T, "b1m": b1m, "b2m": b2m,
            "maskT": maskT, "cmask": cm, "onesb": onesb, "onesf": onesf, "onesfr": onesfr,
            "idn": idn,
        })

    res = run_bass_kernel_spmd(nc, in_maps, core_ids=list(range(8)))
    out = np.empty((2, TFULL, D), np.float32)
    for c in range(8):
        b, j = c // 4, c % 4
        out[b, j * TOWN:(j + 1) * TOWN] = np.asarray(res.results[c]["out_f"]).T
    if DEBUG_TAPS:
        kernel.last_results = res
    return out
